# version 34
# baseline (speedup 1.0000x reference)
import numpy as np
import ml_dtypes
import concourse.bass as bass
import concourse.mybir as mybir
from concourse.bass_utils import run_bass_kernel_spmd

F32 = mybir.dt.float32
BF16 = mybir.dt.bfloat16
F16 = mybir.dt.float16
U8 = mybir.dt.uint8
AF = mybir.ActivationFunctionType
ALU = mybir.AluOpType
AX = mybir.AxisListType
BFNP = ml_dtypes.bfloat16

NCORES = 8
S = 2048
D = 1024
H = 8
NT = 16
NC = 8
SBIG = 1.0e8
FOLDS = [128, 64, 32, 16, 8, 4, 2, 1]
VRS = 272

ENGINES = ("tensor", "vector", "scalar", "gpsimd", "sync")
V, SC, G, T, SY = "vector", "scalar", "gpsimd", "tensor", "sync"


class Event:
    __slots__ = ("sem", "value")
    def __init__(self, sem, value):
        self.sem = sem
        self.value = value


class _Sem:
    __slots__ = ("handle", "count", "name")
    def __init__(self, handle, name):
        self.handle = handle
        self.count = 0
        self.name = name


class Buf:
    __slots__ = ("w", "rs", "name")
    def __init__(self, name=""):
        self.w = None
        self.rs = []
        self.name = name


NDMALANES = 6


class Prog:
    def __init__(self, nc):
        self.nc = nc
        self.sems = {e: _Sem(nc.alloc_semaphore(f"p_{e}"), e) for e in ENGINES}
        self.dlanes = {e: [_Sem(nc.alloc_semaphore(f"d_{e}{i}"), f"d_{e}{i}")
                          for i in range(NDMALANES)]
                       for e in (SY, G, SC)}
        self.drr = {e: 0 for e in (SY, G, SC)}
        self.ops = {e: [] for e in ENGINES}
        self.n_ops = 0

    def _gather(self, reads, writes, waits):
        ws = list(waits)
        for b in reads:
            if b.w is not None:
                ws.append(b.w)
        for b in writes:
            if b.w is not None:
                ws.append(b.w)
            ws.extend(b.rs)
        return ws

    def _add(self, eng, emit, ws, inc):
        s = self.sems[eng]
        ev = None
        if inc:
            s.count += inc
            ev = Event(s, s.count)
        self.ops[eng].append((tuple(w for w in ws if w is not None), emit, ev, inc))
        self.n_ops += 1
        return ev

    def _commit(self, ev, reads, writes):
        for b in reads:
            b.rs.append(ev)
        for b in writes:
            b.w = ev
            b.rs = []

    def op(self, eng, emit, reads=(), writes=(), waits=()):
        ev = self._add(eng, emit, self._gather(reads, writes, waits), 1)
        self._commit(ev, reads, writes)
        return ev

    def dma(self, eng, out, in_, reads=(), writes=(), waits=(), tr=False):
        lane = self.dlanes[eng][self.drr[eng]]
        self.drr[eng] = (self.drr[eng] + 1) % NDMALANES
        ws = self._gather(reads, writes, waits)
        if lane.count:
            ws.append(Event(lane, lane.count))
        if tr:
            def emit(e, out=out, in_=in_):
                return e.dma_start_transpose(out, in_)
        else:
            def emit(e, out=out, in_=in_):
                return e.dma_start(out=out, in_=in_)
        lane.count += 16
        ev = Event(lane, lane.count)
        self.ops[eng].append((tuple(w for w in ws if w is not None),
                              emit, ev, 16))
        self.n_ops += 1
        self._commit(ev, reads, writes)
        return ev

    def dma_total_events(self):
        evs = []
        for e in (SY, G, SC):
            for lane in self.dlanes[e]:
                if lane.count:
                    evs.append(Event(lane, lane.count))
        return evs

    def raw(self, eng, emit, waits=()):
        return self._add(eng, emit, list(waits), 0)

    def build(self, block):
        for eng_name in ENGINES:
            if not self.ops[eng_name]:
                continue
            def run(eng, eng_name=eng_name):
                seen = {}
                my = self.sems[eng_name]
                for waits, emit, ev, inc in self.ops[eng_name]:
                    need = {}
                    for w in waits:
                        if w.sem is my and eng_name in ("sync", "tensor"):
                            continue
                        k = id(w.sem)
                        if w.value > need.get(k, (None, -1))[1]:
                            need[k] = (w.sem, w.value)
                    for sem, val in need.values():
                        if seen.get(id(sem), -1) >= val:
                            continue
                        eng.wait_ge(sem.handle, val)
                        seen[id(sem)] = val
                    ins = emit(eng)
                    if ev is not None:
                        ins.then_inc(ev.sem.handle, inc)
            getattr(block, eng_name)(run)


_TABLES = None

def _host_tables():
    global _TABLES
    if _TABLES is not None:
        return _TABLES
    m_idx = np.arange(64, dtype=np.float64)
    theta = 10000.0 ** (-m_idx / 63.0)
    t_idx = np.arange(S, dtype=np.float64)
    ang = t_idx[:, None] * theta[None, :]
    cos_fm = np.ascontiguousarray(np.repeat(np.cos(ang), 2, axis=1).T.astype(np.float32))
    sin_fm = np.ascontiguousarray(np.repeat(np.sin(ang), 2, axis=1).T.astype(np.float32))
    d_h = 1.0 - 2.0 ** (-5.0 - np.arange(H, dtype=np.float64))
    dpow = d_h[:, None] ** t_idx[None, :]
    msum = (1.0 - d_h[:, None] ** (t_idx[None, :] + 1.0)) / (1.0 - d_h[:, None])
    a_tab = (dpow / np.sqrt(msum)).astype(np.float32)
    b_tab = ((1.0 / dpow) * (128.0 ** -0.5)).astype(np.float32)
    ab_a = np.ascontiguousarray(np.broadcast_to(a_tab[:, None, :], (H, 128, S)))
    ab_b = np.ascontiguousarray(np.broadcast_to(b_tab[:, None, :], (H, 128, S)))
    P = np.zeros((128, 128), dtype=np.float32)
    for m2 in range(64):
        P[2 * m2, 2 * m2 + 1] = -1.0
        P[2 * m2 + 1, 2 * m2] = 1.0
    _TABLES = dict(
        cosfm=cos_fm, sinfm=sin_fm,
        aba=ab_a.astype(np.float32), abb=ab_b.astype(np.float32),
        Pt=np.ascontiguousarray(P.T),
        tri=np.ascontiguousarray(
            (np.arange(128)[:, None] <= np.arange(128)[None, :]).astype(np.float32)),
        ident=np.eye(128, dtype=np.float32))
    return _TABLES


def build_nc(stop_stage=None, debug=False):
    nc = bass.Bass("TRN2", target_bir_lowering=False)

    d_xtm = nc.dram_tensor("x_tm", [S, D], F32, kind="ExternalInput").ap()
    d_xfm = nc.dram_tensor("x_fm", [D, S], F32, kind="ExternalInput").ap()
    d_wq = nc.dram_tensor("wq", [D, D], F32, kind="ExternalInput").ap()
    d_wk = nc.dram_tensor("wk", [D, D], F32, kind="ExternalInput").ap()
    d_wv = nc.dram_tensor("wv", [D, 2 * D], F32, kind="ExternalInput").ap()
    d_wg = nc.dram_tensor("wg", [D, 2 * D], F32, kind="ExternalInput").ap()
    d_wo = nc.dram_tensor("wo", [2 * D, D], F32, kind="ExternalInput").ap()
    d_rhsT = nc.dram_tensor("rhsT", [D, D], F32, kind="ExternalInput").ap()
    d_rhs = nc.dram_tensor("rhs", [D, D], F32, kind="ExternalInput").ap()
    d_cos = nc.dram_tensor("cosfm", [128, S], F32, kind="ExternalInput").ap()
    d_sin = nc.dram_tensor("sinfm", [128, S], F32, kind="ExternalInput").ap()
    d_aba = nc.dram_tensor("aba", [H, 128, S], F32, kind="ExternalInput").ap()
    d_abb = nc.dram_tensor("abb", [H, 128, S], F32, kind="ExternalInput").ap()
    d_Pt = nc.dram_tensor("Pt", [128, 128], F32, kind="ExternalInput").ap()
    d_tri = nc.dram_tensor("tri", [128, 128], F32, kind="ExternalInput").ap()
    d_id = nc.dram_tensor("ident", [128, 128], F32, kind="ExternalInput").ap()
    d_out = nc.dram_tensor("out", [S, D], F32, kind="ExternalOutput").ap()

    _off = [0]
    def region(nbytes, align=64):
        o = (_off[0] + align - 1) // align * align
        _off[0] = o + nbytes
        return o
    _arena = [None]
    def view(off, cols, dtype, rows=128):
        esz = {F32: 4, BF16: 2, F16: 2, U8: 1}[dtype]
        return _arena[0][0:rows, off:off + cols * esz].bitcast(dtype)

    o_xn = region(32768)
    o_qx = region(16384)
    o_z1 = region(32768)
    o_z2 = region(32768)
    o_cs = region(8192)
    o_bc = region(4096)
    o_qr = region(8192)
    o_ab = region(8192)
    o_qd = region(4096)
    o_kd = region(4096)
    o_qs = region(4096)
    o_vr = region(2 * 16 * VRS)
    o_g1 = region(8192)
    o_sc = region(2048)
    o_wqk = region(4096)
    o_wvg = region(8192)
    o_xs = region(4096)
    o_uf = region(2048)
    o_uk = region(2048)
    o_ktm = region(4096)
    o_ts = region(2048)
    o_rt = region(2048)
    o_row = region(2048)
    o_stc = region(512)
    o_st2 = region(512)
    o_st3 = region(256)
    o_rsc = region(64)
    o_cc = region(64)
    o_pt = region(256)
    o_tri = region(256)
    o_idb = region(256)
    o_idf = region(512)
    o_or = region(512)
    o_ep = region(2048)
    o_ec = region(128)
    o_zs = region(2048)
    ARENA = (_off[0] + 63) // 64 * 64
    avail = nc.sbuf_bytes_remaining
    assert ARENA <= avail, (ARENA, avail)
    _arena[0] = nc.alloc_sbuf_tensor("arena", [128, ARENA], U8).ap()

    xn = view(o_xn, NC * S, BF16)
    wo_b = view(o_xn, 16 * D, BF16)
    qx8 = view(o_qx, NC * S, U8)
    z1 = view(o_z1, 8 * S, BF16)
    z2 = view(o_z2, 8 * S, BF16)
    cs = view(o_cs, 2 * S, BF16)
    bc = view(o_bc, 1024, F32)
    qr8 = view(o_qr, 8192, U8)
    ab = view(o_ab, 2 * S, BF16)
    qd = view(o_qd, S, BF16)
    kd = view(o_kd, S, BF16)
    qsb = view(o_qs, S, BF16)
    vr = view(o_vr, 2 * 16 * VRS // 2, BF16)
    g1 = view(o_g1, 16 * 256, BF16)
    scsb = view(o_sc, 1024, BF16)
    wqk = view(o_wqk, 2048, BF16)
    wvg = view(o_wvg, 4096, BF16)
    xs = view(o_xs, 1024, F32)
    uf = view(o_uf, 512, F32)
    uk = view(o_uk, 512, F32)
    ufuk = view(o_uf, 1024, F32)
    b0h = view(o_uf, 1024, F16)
    bwh = view(o_uk, 1024, F16)
    thx = [view(o_xs, 1024, F16), view(o_bc, 1024, F16)]
    xqx = [view(o_xs, 1024, F32), view(o_bc, 1024, F32)]
    rowq = view(o_row, 1024, BF16)
    qsf = view(o_qs, 1024, F32)
    ktm = view(o_ktm, 16 * 128, BF16)
    tst = view(o_ts, 1024, BF16)
    rtt = view(o_rt, 1024, BF16)
    row = view(o_row, 512, F32, rows=1)
    stc = view(o_stc, 64, F32)
    st2 = view(o_st2, 64, F32)
    st3 = view(o_st3, 32, F32)
    rscol = view(o_rsc, 16, F32)
    cc = view(o_cc, 16, F32)
    ptb = view(o_pt, 128, BF16)
    trib = view(o_tri, 128, BF16)
    idb = view(o_idb, 128, BF16)
    idf = view(o_idf, 128, F32)
    onr = view(o_or, 128, F32, rows=1)
    ep = view(o_ep, 1024, BF16)
    ec = view(o_ec, 32, F32)
    zst = view(o_zs, 1024, BF16)
    qrb = view(o_vr, 8192, BF16)
    qxb = view(o_qs, S, BF16)
    g2t = view(o_qd, 2048, BF16)
    rsb2 = view(o_bc, 1024, F32)
    xrl = view(o_xs, 1024, F32)

    fin = view(o_ab, 1024, F32)
    fot = view(o_ab + 4096, 1024, F32)

    ps_a = nc.alloc_psum_tensor("psA", [128, 512], F32).ap()
    ps_b = nc.alloc_psum_tensor("psB", [128, 512], F32).ap()
    ps_g = nc.alloc_psum_tensor("psG", [128, 2048], F32).ap()
    ps_x = nc.alloc_psum_tensor("psX", [128, 512], F32).ap()
    ps_t = nc.alloc_psum_tensor("psT", [128, 1024], BF16).ap()

    p = Prog(nc)

    _taps = {}
    def tap(name, ap, reads):
        if not debug or (isinstance(debug, (set, frozenset, list, tuple))
                         and name not in debug):
            return
        shp = list(ap.shape)
        dt_ = ap.dtype
        o_ = nc.dram_tensor("tap_" + name, shp, dt_, kind="ExternalOutput").ap()
        p.dma(SY, o_, ap, reads=list(reads))

    b_xng = [Buf("xn0"), Buf("xn1"), Buf("xn2"), Buf("xn3")]
    b_qx8 = Buf("qx8")
    b_z = [Buf(f"z{i}") for i in range(16)]
    b_cs = Buf("cs"); b_bc = Buf("bc"); b_qr8 = Buf("qr8"); b_ab = Buf("ab")
    b_qd4 = [Buf(f"qd{t}") for t in range(4)]
    b_kd4 = [Buf(f"kd{t}") for t in range(4)]
    b_ktm = Buf("ktm")
    b_qsb = [Buf("qsb0"), Buf("qsb1")]
    b_rot = [Buf("rot0"), Buf("rot1")]
    b_vr = Buf("vr"); b_g1 = Buf("g1")
    b_scsb = [Buf("sc0"), Buf("sc1")]
    b_wqk = Buf("wqk"); b_wvg = Buf("wvg")
    b_xs = [Buf("xs0"), Buf("xs1")]
    b_uf = [Buf("uf0"), Buf("uf1")]
    b_uk = [Buf("uk0"), Buf("uk1")]
    b_tst = [Buf("ts0"), Buf("ts1")]
    b_rtt = [Buf("rt0"), Buf("rt1")]
    b_row = Buf("row")
    b_stc = Buf("stc"); b_st2 = Buf("st2"); b_st3 = Buf("st3")
    b_rsc = Buf("rscol")
    b_cc = Buf("cc"); b_tabs = Buf("tabs")
    b_ep = Buf("ep"); b_ec = Buf("ec"); b_ec2 = Buf("ec2"); b_zst = Buf("zst")
    b_psA = Buf("psA"); b_psB = Buf("psB"); b_psG = Buf("psG")
    b_psX = Buf("psX"); b_psT = Buf("psT")
    b_wo = Buf("wo_b"); b_qrb = Buf("qrb")
    b_qxb = [Buf("qxb0"), Buf("qxb1")]
    b_g2 = [Buf("g2a"), Buf("g2b")]

    def em_consts(v):
        v.memset(cc[:, 0:1], SBIG)
        v.memset(cc[:, 9:10], 128.5)
        v.memset(cc[:, 10:11], -128.0)
        ins = None
        for i, b in enumerate(FOLDS):
            ins = v.memset(cc[:, 1 + i:2 + i], -SBIG * float(b))
        return ins
    p.op(V, em_consts, writes=[b_cc])
    p.op(V, lambda v: v.memset(onr[:, :], 1.0), writes=[b_tabs])
    def em_vrones(v):
        ins = None
        for i in range(16):
            ins = v.memset(vr[:, VRS * i + 256:VRS * i + 257], 1.0)
            ins = v.memset(vr[:, VRS * i + 257:VRS * i + 260], 0.0)
        return ins
    p.op(V, em_vrones, writes=[b_vr])
    p.dma(G, cs[:, 0:S], d_cos, writes=[b_cs])
    p.dma(G, cs[:, S:2 * S], d_sin, writes=[b_cs])
    p.dma(G, ptb[:, :], d_Pt, writes=[b_tabs])
    p.dma(G, trib[:, :], d_tri, writes=[b_tabs])
    p.dma(G, idb[:, :], d_id, writes=[b_tabs])
    p.dma(SY, idf[:, :], d_id, writes=[b_tabs])

    _xpb = [b_xs[0], b_xs[1]]
    _bcb = [b_bc]
    def _xpw(slot):
        return _xpb if slot == 0 else _bcb
    def _stage(slot):
        return (tst, b_tst[0]) if slot == 0 else (rowq, b_row)

    _p0bufs = [
        (xs[:, 0:1024], [b_xs[0], b_xs[1]]),
        (xqx[1], [b_bc]),
        (ufuk[:, 0:1024], [b_uf[0], b_uk[0]]),
        (qsf, [b_qsb[0], b_qsb[1], b_rot[0], b_rot[1]]),
    ]

    def _statmath(tg):
        a = 4 * tg
        m_ = st2[:, a:a + 4]
        rs_ = st2[:, 16 + a:16 + a + 4]
        ls_ = st2[:, 32 + a:32 + a + 4]
        sc_ = st2[:, 48 + a:48 + a + 4]
        e_ = ec[:, a:a + 4]
        p.op(V, lambda v: v.tensor_scalar(m_, stc[:, a:a + 4], 1.0 / D, None,
                                          ALU.mult),
             reads=[b_stc], writes=[b_st2])
        p.op(V, lambda v: v.tensor_mul(sc_, m_, m_),
             reads=[b_st2], writes=[b_st2])
        p.op(V, lambda v: v.scalar_tensor_tensor(
            sc_, stc[:, 16 + a:16 + a + 4], 1.0 / D, sc_,
            ALU.mult, ALU.subtract), reads=[b_stc, b_st2], writes=[b_st2])
        p.op(V, lambda v: v.tensor_scalar(sc_, sc_, 1e-5, None, ALU.add),
             reads=[b_st2], writes=[b_st2])
        p.op(V, lambda v: v.reciprocal(rs_, sc_),
             reads=[b_st2], writes=[b_st2])
        p.op(SC, lambda s: s.activation(rs_, rs_, AF.Sqrt),
             reads=[b_st2], writes=[b_st2])
        p.op(V, lambda v: v.tensor_sub(sc_, stc[:, 32 + a:32 + a + 4], m_),
             reads=[b_stc, b_st2], writes=[b_st2])
        p.op(V, lambda v: v.tensor_sub(e_, m_, stc[:, 48 + a:48 + a + 4]),
             reads=[b_stc, b_st2], writes=[b_ec])
        p.op(V, lambda v: v.tensor_max(sc_, sc_, e_),
             reads=[b_ec], writes=[b_st2])
        p.op(V, lambda v: v.tensor_mul(sc_, sc_, rs_),
             reads=[b_st2], writes=[b_st2])
        p.op(V, lambda v: v.tensor_scalar(sc_, sc_, 1e-12, None, ALU.max),
             reads=[b_st2], writes=[b_st2])
        p.op(V, lambda v: v.tensor_scalar(ls_, sc_, 1.0 / 127.0, None,
                                          ALU.mult),
             reads=[b_st2], writes=[b_st2])
        p.op(V, lambda v: v.reciprocal(e_, sc_),
             reads=[b_st2], writes=[b_ec])
        p.op(V, lambda v: v.scalar_tensor_tensor(
            st3[:, a:a + 4], e_, 127.0, rs_, ALU.mult, ALU.mult),
            reads=[b_ec, b_st2], writes=[b_st3])
        p.op(V, lambda v: v.tensor_mul(st3[:, 16 + a:16 + a + 4], m_,
                                       st3[:, a:a + 4]),
             reads=[b_st2, b_st3], writes=[b_st3])
        p.op(V, lambda v: v.tensor_scalar(
            st3[:, 16 + a:16 + a + 4], st3[:, 16 + a:16 + a + 4],
            -128.5, None, ALU.add), reads=[b_st3], writes=[b_st3])
        p.op(V, lambda v: v.tensor_mul(sc_, m_, rs_),
             reads=[b_st2], writes=[b_st2])

    for tg in range(4):
        for k in range(4):
            i = 4 * tg + k
            sl, bsl = _p0bufs[k]
            p.dma(SY, sl, d_xtm[128 * i:128 * i + 128, :], writes=bsl)
            p.op(V, (lambda sl=sl, i=i: lambda v: v.reduce_max(
                stc[:, 32 + i:33 + i], sl, axis=AX.X))(),
                reads=bsl, writes=[b_stc])
            p.op(V, (lambda sl=sl, i=i: lambda v: v.tensor_reduce(
                stc[:, 48 + i:49 + i], sl, axis=AX.X, op=ALU.min))(),
                reads=bsl, writes=[b_stc])
            p.op(V, (lambda sl=sl, i=i: lambda v: v.reduce_sum(
                stc[:, i:i + 1], sl, axis=AX.X))(),
                reads=bsl, writes=[b_stc])
            p.op(SC, (lambda sl=sl: lambda s: s.activation(
                ep[:, 0:1024], sl, AF.Square))(),
                reads=bsl, writes=[b_ep])
            p.op(V, (lambda i=i: lambda v: v.reduce_sum(
                stc[:, 16 + i:17 + i], ep[:, 0:1024], axis=AX.X))(),
                reads=[b_ep], writes=[b_stc])
        _statmath(tg)
        for k in range(4):
            i = 4 * tg + k
            sl, bsl = _p0bufs[k]
            stg, bstg = _stage(k % 2)
            p.op(V, (lambda i=i, sl=sl, stg=stg: lambda v: v.tensor_scalar(
                stg[:, 0:1024], sl,
                st2[:, 16 + i:17 + i], st2[:, 48 + i:49 + i],
                ALU.mult, ALU.subtract))(),
                reads=bsl + [b_st2], writes=[bstg])
            for j in range(8):
                p.op(T, (lambda j=j, stg=stg: lambda t: t.transpose(
                    ps_t[:, 128 * j:128 * j + 128],
                    stg[:, 128 * j:128 * j + 128], idb[:, :]))(),
                    reads=[bstg, b_tabs], writes=[b_psT])
            dst = xn.rearrange("p (c t) -> p c t", c=NC)[
                :, :, 128 * i:128 * i + 128]
            src = ps_t[:, 0:1024].rearrange("p (j c) -> p j c", j=8)
            p.op(SC, (lambda dst=dst, src=src: lambda s: s.copy(dst, src))(),
                 reads=[b_psT], writes=[b_xng[tg]])

    def emit_col2bcast(colap, breads, dst_off, tg):
        for j in range(4):
            i = 4 * tg + j
            p.op(T, (lambda i=i, j=j, colap=colap: lambda t: t.matmul(
                ps_x[0:1, 128 * j:128 * j + 128], colap[:, i:i + 1], idf[:, :],
                start=True, stop=True))(),
                reads=list(breads) + [b_tabs], writes=[b_psX])
        p.op(SC, lambda s2: s2.copy(row[:, 0:512], ps_x[0:1, 0:512]),
             reads=[b_psX], writes=[b_row])
        p.op(T, lambda t: t.matmul(ps_x[:, 0:512], onr[:, :], row[:, :],
                                   start=True, stop=True),
             reads=[b_row, b_tabs], writes=[b_psX])
        p.op(SC, (lambda dst_off=dst_off: lambda s2: s2.copy(
            bc[:, dst_off:dst_off + 512], ps_x[:, 0:512]))(),
            reads=[b_psX], writes=[b_bc])

    def _finish():
        while pending:
            pending.pop(0)()
        for fn in epi_queue:
            fn()
        del epi_queue[:]
        for fn in tp_pending:
            fn()
        del tp_pending[:]
        p.raw(G, lambda g: g.nop(), waits=p.dma_total_events())
        all_sems = [s.handle for s in p.sems.values()]
        for lanes in p.dlanes.values():
            all_sems.extend(l.handle for l in lanes)
        for sh in all_sems:
            nc.gpsimd.sem_clear(sh)
        nc.all_engine_barrier()
        with nc.Block() as block:
            p.build(block)
        return nc

    tap("xn", xn[:, :], b_xng)
    tap("bc", bc[:, :], [b_bc])

    pending = []

    _spg = [True]
    def sprinkle(n):
        if not _spg[0]:
            return
        k = 0
        while pending and k < n:
            pending.pop(0)()
            k += 1

    def emit_fold_chain(ufl, tsl, buf_, bts_, eng=V):
        for bi, bv in enumerate(FOLDS):
            p.op(eng, (lambda bv=bv, ufl=ufl, tsl=tsl: lambda e:
                       e.tensor_scalar(
                tsl, ufl, float(bv), -float(bv), ALU.is_ge, ALU.mult))(),
                reads=[buf_], writes=[bts_])
            p.op(eng, (lambda ufl=ufl, tsl=tsl: lambda e: e.tensor_add(
                ufl, ufl, tsl))(),
                reads=[bts_], writes=[buf_])

    pidx = [0]

    def _fold_chain(slot, buf_, buk_):
        th = thx[slot]
        bx = _xpw(slot)
        first = True
        for bv in FOLDS:
            bsrc = b0h if first else bwh
            p.op(V, (lambda bv=bv, bsrc=bsrc, th=th: lambda v:
                     v.tensor_scalar(
                th[:, 0:1024], bsrc[:, 0:1024], float(bv), -float(bv),
                ALU.is_ge, ALU.mult))(),
                reads=[buf_ if first else buk_], writes=bx)
            p.op(V, (lambda bsrc=bsrc, th=th: lambda v: v.tensor_add(
                bwh[:, 0:1024], bsrc[:, 0:1024], th[:, 0:1024]))(),
                reads=bx + ([buf_] if first else []), writes=[buk_])
            first = False

    def emit_xq_v(i):
        slot = i % 2
        xsl = xqx[slot]
        bx = _xpw(slot)
        buf_, buk_ = b_uf[0], b_uk[0]
        stg, bstg = _stage(slot)
        p.dma(SY, xsl, d_xtm[128 * i:128 * i + 128, :], writes=bx)
        p.op(V, (lambda i=i, xsl=xsl: lambda v: v.tensor_scalar(
            b0h[:, 0:1024], xsl, st3[:, i:i + 1], st3[:, 16 + i:17 + i],
            ALU.mult, ALU.subtract))(),
            reads=bx + [b_st3], writes=[buf_])
        _fold_chain(slot, buf_, buk_)
        p.op(V, (lambda stg=stg: lambda v: v.tensor_sub(
            stg[:, 0:1024], b0h[:, 0:1024], bwh[:, 0:1024]))(),
            reads=[buf_, buk_], writes=[bstg])

    def emit_xq_t(i):
        slot = i % 2
        stg, bstg = _stage(slot)
        for j in range(8):
            p.op(T, (lambda j=j, stg=stg: lambda t: t.transpose(
                ps_t[:, 128 * j:128 * j + 128],
                stg[:, 128 * j:128 * j + 128], idb[:, :]))(),
                reads=[bstg, b_tabs], writes=[b_psT])
        dst = qx8.rearrange("p (c t) -> p c t", c=NC)[
            :, :, 128 * i:128 * i + 128]
        src = ps_t[:, 0:1024].rearrange("p (j c) -> p j c", j=8)
        p.op(SC, (lambda dst=dst, src=src: lambda s: s.copy(dst, src))(),
             reads=[b_psT], writes=[b_qx8])

    def emit_rhsA(c):
        bxs = [b_xs[0], b_xs[1]]
        xsl = xs[:, 0:1024]
        p.dma(SY, xsl, d_rhsT[128 * c:128 * c + 128, :], writes=bxs)
        p.op(V, (lambda c=c, xsl=xsl: lambda v: v.reduce_max(
            ec[:, 24:25], xsl, axis=AX.X, apply_absolute_value=True))(),
            reads=bxs, writes=[b_ec2])
        p.op(V, (lambda c=c: lambda v: v.tensor_scalar(
            rscol[:, c:c + 1], ec[:, 24:25], 1e-12, 1.0 / 127.0,
            ALU.max, ALU.mult))(),
            reads=[b_ec2], writes=[b_rsc])

    def emit_rhsB_v(cf):
        slot = cf % 2
        xsl = xqx[slot]
        bx = _xpw(slot)
        buf_, buk_ = b_uf[0], b_uk[0]
        stg, bstg = _stage(slot)
        p.dma(SY, xsl, d_rhsT[128 * cf:128 * cf + 128, :], writes=bx)
        p.op(V, (lambda cf=cf, xsl=xsl: lambda v: v.tensor_scalar(
            b0h[:, 0:1024], xsl,
            rscol[:, 8 + cf:9 + cf], cc[:, 9:10],
            ALU.mult, ALU.add))(),
            reads=bx + [b_rsc, b_cc], writes=[buf_])
        _fold_chain(slot, buf_, buk_)
        p.op(V, (lambda stg=stg: lambda v: v.tensor_sub(
            stg[:, 0:1024], b0h[:, 0:1024], bwh[:, 0:1024]))(),
            reads=[buf_, buk_], writes=[bstg])

    def emit_rhsB_t(cf):
        slot = cf % 2
        stg, bstg = _stage(slot)
        for j in range(8):
            p.op(T, (lambda j=j, stg=stg: lambda t: t.transpose(
                ps_t[:, 128 * j:128 * j + 128],
                stg[:, 128 * j:128 * j + 128], idb[:, :]))(),
                reads=[bstg, b_tabs], writes=[b_psT])
        dst = qr8.rearrange("p (c f) -> p c f", c=NC)[
            :, :, 128 * cf:128 * cf + 128]
        src = ps_t[:, 0:1024].rearrange("p (j c) -> p j c", j=8)
        p.op(SC, (lambda dst=dst, src=src: lambda s: s.copy(dst, src))(),
             reads=[b_psT], writes=[b_qr8])

    def emit_invrs_bcast():
        p.op(V, lambda v: v.reciprocal(ec[:, 16:24], rscol[:, 0:8]),
             reads=[b_rsc], writes=[b_ec2])
        for c in range(NC):
            p.op(T, (lambda c=c: lambda t: t.matmul(
                ps_x[0:1, 0:128], ec[:, 16 + c:17 + c], idf[:, :],
                start=True, stop=True))(),
                reads=[b_ec2, b_tabs], writes=[b_psX])
            p.op(SC, lambda s2: s2.copy(row[:, 0:128], ps_x[0:1, 0:128]),
                 reads=[b_psX], writes=[b_row])
            p.op(T, lambda t: t.matmul(ps_x[:, 0:128], onr[:, :],
                                       row[:, 0:128], start=True, stop=True),
                 reads=[b_row, b_tabs], writes=[b_psX])
            p.op(SC, (lambda c=c: lambda s2: s2.copy(
                bc[:, 128 * c:128 * c + 128], ps_x[:, 0:128]))(),
                reads=[b_psX], writes=[b_bc])

    for c in range(NC):
        pending.append((lambda c=c: lambda: emit_rhsA(c))())
    pending.append(lambda: p.op(V, lambda v: v.reciprocal(
        rscol[:, 8:16], rscol[:, 0:8]), reads=[b_rsc], writes=[b_rsc]))
    for i in range(NT):
        pending.append((lambda i=i: lambda: emit_xq_v(i))())
        if i > 0:
            pending.append((lambda i=i: lambda: emit_xq_t(i - 1))())
    pending.append(lambda: emit_xq_t(NT - 1))
    for cf in range(NC):
        pending.append((lambda cf=cf: lambda: emit_rhsB_v(cf))())
        if cf > 0:
            pending.append((lambda cf=cf: lambda: emit_rhsB_t(cf - 1))())
    pending.append(lambda: emit_rhsB_t(NC - 1))

    pp = [(ps_b, b_psB), (ps_x, b_psX)]
    ping = [0]
    def nxt():
        ping[0] ^= 1
        return pp[ping[0]]

    def load_head_weights(h):
        dst = wqk[:, 0:1024].rearrange("p (c f) -> p c f", c=8)
        src = d_wq.rearrange("(c p) f -> p c f", p=128)[:, :, 128 * h:128 * h + 128]
        p.dma(G, dst, src, writes=[b_wqk])
        dst = wqk[:, 1024:2048].rearrange("p (c f) -> p c f", c=8)
        src = d_wk.rearrange("(c p) f -> p c f", p=128)[:, :, 128 * h:128 * h + 128]
        p.dma(G, dst, src, writes=[b_wqk])
        dst = wvg[:, 0:2048].rearrange("p (c f) -> p c f", c=8)
        src = d_wv.rearrange("(c p) f -> p c f", p=128)[:, :, 256 * h:256 * h + 256]
        p.dma(G, dst, src, writes=[b_wvg])
        dst = wvg[:, 2048:4096].rearrange("p (c f) -> p c f", c=8)
        src = d_wg.rearrange("(c p) f -> p c f", p=128)[:, :, 256 * h:256 * h + 256]
        p.dma(G, dst, src, writes=[b_wvg])
        p.dma(G, ab[:, 0:S], d_aba[h], writes=[b_ab])
        p.dma(G, ab[:, S:2 * S], d_abb[h], writes=[b_ab])

    load_head_weights(0)
    tp_pending = []
    epi_queue = []

    if stop_stage == "p0":
        return _finish()

    for h in range(H):
        _spg[0] = h >= 2
        for fn in epi_queue:
            fn()
        del epi_queue[:]
        for i2 in range(0, NT, 2):
            ps, bps = nxt()
            def em_vmm(i2=i2, ps=ps):
                def f(t):
                    ins = None
                    for k in range(2):
                        i = i2 + k
                        for c in range(NC):
                            ins = t.matmul(
                                ps[:, 256 * k:256 * k + 256],
                                xn[:, S * c + 128 * i:S * c + 128 * i + 128],
                                wvg[:, 256 * c:256 * c + 256],
                                start=(c == 0), stop=(c == NC - 1))
                    return ins
                return f
            p.op(T, em_vmm(), reads=[b_xng[i2 // 4], b_wvg], writes=[bps])
            p.op(SC, (lambda i2=i2, ps=ps: lambda s: s.copy(
                vr.rearrange("p (s t) -> p s t", s=16)[:, i2:i2 + 2, 0:256],
                ps[:, 0:512].rearrange("p (s t) -> p s t", s=2)))(),
                reads=[bps], writes=[b_vr])
            if i2 == 2 and tp_pending:
                for fn in tp_pending:
                    fn()
                del tp_pending[:]
            if i2 % 4 == 2:
                sprinkle(1)
        for i2 in range(0, NT, 2):
            ps, bps = nxt()
            def em_gmm(i2=i2, ps=ps):
                def f(t):
                    ins = None
                    for k in range(2):
                        i = i2 + k
                        for c in range(NC):
                            ins = t.matmul(
                                ps[:, 256 * k:256 * k + 256],
                                xn[:, S * c + 128 * i:S * c + 128 * i + 128],
                                wvg[:, 2048 + 256 * c:2048 + 256 * c + 256],
                                start=(c == 0), stop=(c == NC - 1))
                    return ins
                return f
            p.op(T, em_gmm(), reads=[b_xng[i2 // 4], b_wvg], writes=[bps])
            p.op(SC, (lambda i2=i2, ps=ps: lambda s: s.activation(
                g1[:, 256 * i2:256 * i2 + 512], ps[:, 0:512], AF.Silu))(),
                reads=[bps], writes=[b_g1])

        for qk in range(2):
            wofs = 1024 * qk
            dstd = qd if qk == 0 else kd
            bdd4 = b_qd4 if qk == 0 else b_kd4
            abo = 0 if qk == 0 else S
            for tg in range(4):
                ps, bps = nxt()
                sslot = tg % 2
                qso = 1024 * sslot
                roto = 1024 * sslot + 512
                bq, br = b_qsb[sslot], b_rot[sslot]
                def em_qmm(tg=tg, ps=ps, wofs=wofs):
                    def f(t):
                        ins = None
                        for c in range(NC):
                            ins = t.matmul(
                                ps[:, 0:512],
                                wqk[:, wofs + 128 * c:wofs + 128 * c + 128],
                                xn[:, S * c + 512 * tg:S * c + 512 * tg + 512],
                                start=(c == 0), stop=(c == NC - 1))
                        return ins
                    return f
                p.op(T, em_qmm(), reads=[b_xng[tg], b_wqk], writes=[bps])
                p.op(SC, (lambda qso=qso, ps=ps: lambda s: s.copy(
                    qsb[:, qso:qso + 512], ps[:, 0:512]))(),
                    reads=[bps], writes=[bq])
                p.op(T, (lambda qso=qso: lambda t: t.matmul(
                    ps_x[:, 0:512], ptb[:, :], qsb[:, qso:qso + 512],
                    start=True, stop=True))(),
                    reads=[bq, b_tabs], writes=[b_psX])
                to = 512 * tg
                rts = rtt[:, 512 * sslot:512 * sslot + 512]
                brt = b_rtt[sslot]
                p.op(V, (lambda qso=qso, to=to, rts=rts: lambda v: v.tensor_mul(
                    rts, qsb[:, qso:qso + 512], cs[:, to:to + 512]))(),
                    reads=[bq, b_cs], writes=[brt])
                p.op(V, (lambda to=to, qso=qso: lambda v: v.tensor_mul(
                    qsb[:, qso:qso + 512], ps_x[:, 0:512],
                    cs[:, S + to:S + to + 512]))(),
                    reads=[b_psX, b_cs], writes=[bq])
                p.op(V, (lambda rts=rts, qso=qso: lambda v: v.tensor_add(
                    rts, rts, qsb[:, qso:qso + 512]))(),
                    reads=[bq], writes=[brt])
                p.op(V, (lambda dstd=dstd, to=to, abo=abo, rts=rts: lambda v:
                         v.tensor_mul(
                    dstd[:, to:to + 512], rts, ab[:, abo + to:abo + to + 512]))(),
                    reads=[brt, b_ab], writes=[bdd4[tg]])
            sprinkle(1)

        for b8 in range(2):
            for k8 in range(8):
                i = 8 * b8 + k8
                p.op(T, (lambda i=i, k8=k8: lambda t: t.transpose(
                    ps_t[:, 128 * k8:128 * k8 + 128],
                    kd[:, 128 * i:128 * i + 128],
                    idb[:, :]))(),
                    reads=[b_kd4[i // 4], b_tabs], writes=[b_psT])
            p.op(SC, (lambda b8=b8: lambda s: s.copy(
                ktm[:, 1024 * b8:1024 * b8 + 1024],
                ps_t[:, 0:1024]))(),
                reads=[b_psT], writes=[b_ktm])

        if h + 1 < H:
            load_head_weights(h + 1)

        zdst = z1 if h < 4 else z2
        for gI in range(4):
            qbase = 512 * gI
            if gI > 0:
                for j in range(4 * (gI - 1), 4 * gI):
                    p.op(T, (lambda j=j: lambda t: t.matmul(
                        ps_a[:, 0:257],
                        ktm[:, 128 * j:128 * j + 128],
                        vr[:, VRS * j:VRS * j + 257],
                        start=(j == 0), stop=(j % 4 == 3),
                        skip_group_check=True))(),
                        reads=[b_ktm, b_vr], writes=[b_psA])
                ssl = (gI - 1) % 2
                st_ap = qsb[:, 512 * ssl:512 * ssl + 257]
                bst = b_qsb[0] if ssl == 0 else b_rot[0]
                p.op(SC, (lambda st_ap=st_ap: lambda s: s.copy(
                    st_ap, ps_a[:, 0:257]))(),
                    reads=[b_psA], writes=[bst])
                def mk_outmm(qbase=qbase, st_ap=st_ap, bst=bst):
                    def go():
                        for ii in range(4):
                            p.op(T, (lambda ii=ii: lambda t: t.matmul(
                                ps_g[:, 512 * ii:512 * ii + 257],
                                qd[:, qbase + 128 * ii:qbase + 128 * ii + 128],
                                st_ap, start=True, stop=False))(),
                                reads=[bst, b_qd4[(qbase // 512)]],
                                writes=[b_psG])
                    return go
                outmm_pending = mk_outmm()
            else:
                outmm_pending = None
            prev_mm2 = None
            for rJ in range(4):
                Jt = 4 * gI + rJ
                co = 128 * rJ
                qoff = qbase + co
                Nq = 512 - co
                ps, bps = nxt()
                p.op(T, (lambda Jt=Jt, qoff=qoff, Nq=Nq, ps=ps: lambda t:
                         t.matmul(ps[:, 0:Nq],
                                  kd[:, 128 * Jt:128 * Jt + 128],
                                  qd[:, qoff:qoff + Nq],
                                  start=True, stop=True))(),
                    reads=[b_kd4[Jt // 4], b_qd4[qoff // 512]],
                    writes=[bps])
                if rJ == 0 and outmm_pending is not None:
                    outmm_pending()
                    outmm_pending = None
                sso = 512 * (rJ % 2)
                bss = b_scsb[rJ % 2]
                p.op(V, (lambda sso=sso, co=co, ps=ps: lambda v:
                         v.tensor_mul(scsb[:, sso + co:sso + co + 128],
                                      ps[:, 0:128], trib[:, :]))(),
                    reads=[bps, b_tabs], writes=[bss])
                if Nq > 128:
                    p.op(SC, (lambda sso=sso, co=co, Nq=Nq, ps=ps:
                              lambda s: s.copy(
                        scsb[:, sso + co + 128:sso + 512],
                        ps[:, 128:Nq]))(),
                        reads=[bps], writes=[bss])
                if prev_mm2 is not None:
                    prev_mm2()
                for _ in range(2):
                    if epi_queue:
                        epi_queue.pop(0)()
                def mk_mm2(sso=sso, bss=bss, Jt=Jt, rJ=rJ, gI=gI):
                    def go():
                        for ii in range(rJ, 4):
                            It = 4 * gI + ii
                            p.op(T, (lambda sso=sso, ii=ii, Jt=Jt, It=It,
                                     gI=gI, rJ=rJ: lambda t: t.matmul(
                                ps_g[:, 512 * ii:512 * ii + 257],
                                scsb[:, sso + 128 * ii:sso + 128 * ii + 128],
                                vr[:, VRS * Jt:VRS * Jt + 257],
                                start=(gI == 0 and rJ == 0),
                                stop=(Jt == It)))(),
                                reads=[bss, b_vr], writes=[b_psG])
                    return go
                prev_mm2 = mk_mm2()
            prev_mm2()
            for fn in tp_pending:
                fn()
            del tp_pending[:]

            gp = gI % 2
            if gp == 0:
                epb, epw = ep, [b_ep]
                ecb, ecw = ec[:, 0:16], [b_ec]
                zsb, zsw = zst, [b_zst]
            else:
                epb, epw = rtt, [b_rtt[0], b_rtt[1]]
                ecb, ecw = ec[:, 16:32], [b_ec2]
                zsb, zsw = qsb[:, 1024:2048], [b_qsb[1], b_rot[1]]
            psg3 = ps_g.rearrange("p (i s) -> p i s", i=4)
            epg3 = epb.rearrange("p (i s) -> p i s", i=4)
            p.op(SC, (lambda epg3=epg3: lambda s: s.copy(
                epg3, psg3[:, :, 0:256]))(),
                reads=[b_psG], writes=epw)
            p.op(SC, (lambda ecb=ecb: lambda s: s.copy(
                ecb[:, 0:4], psg3[:, :, 256:257]))(),
                reads=[b_psG], writes=ecw)

            def mk_epi(epb=epb, epw=epw, ecb=ecb, ecw=ecw, zsb=zsb, zsw=zsw,
                       gI=gI, h=h, zdst=zdst):
                pieces = []
                pieces.append(lambda: (
                    p.op(V, lambda v: v.scalar_tensor_tensor(
                        ecb[:, 0:4], ecb[:, 0:4], -1.0, ecb[:, 0:4],
                        ALU.mult, ALU.max), reads=[], writes=ecw),
                    p.op(V, lambda v: v.tensor_scalar(
                        ecb[:, 0:4], ecb[:, 0:4], 1.0, None, ALU.max),
                        reads=[], writes=ecw),
                    p.op(V, lambda v: v.reciprocal(ecb[:, 4:8], ecb[:, 0:4]),
                         reads=ecw, writes=ecw)))
                pieces.append(lambda: (
                    p.op(V, lambda v: v.tensor_mul(zsb[:, :], epb[:, :],
                                                   epb[:, :]),
                         reads=epw, writes=zsw),
                    p.op(V, lambda v: v.tensor_reduce(
                        ecb[:, 8:12],
                        zsb[:, :].rearrange("p (i s) -> p i s", i=4),
                        axis=AX.X, op=ALU.add),
                        reads=zsw, writes=ecw)))
                pieces.append(lambda: (
                    p.op(V, lambda v: v.tensor_mul(ecb[:, 12:16], ecb[:, 4:8],
                                                   ecb[:, 4:8]),
                         reads=ecw, writes=ecw),
                    p.op(V, lambda v: v.tensor_mul(ecb[:, 12:16],
                                                   ecb[:, 12:16],
                                                   ecb[:, 8:12]),
                         reads=[], writes=ecw),
                    p.op(V, lambda v: v.tensor_scalar(
                        ecb[:, 12:16], ecb[:, 12:16], 1.0 / 256.0, 1e-5,
                        ALU.mult, ALU.add), reads=[], writes=ecw),
                    p.op(V, lambda v: v.reciprocal(ecb[:, 12:16],
                                                   ecb[:, 12:16]),
                         reads=[], writes=ecw),
                    p.op(SC, lambda s: s.activation(ecb[:, 12:16],
                                                    ecb[:, 12:16], AF.Sqrt),
                         reads=ecw, writes=ecw),
                    p.op(V, lambda v: v.tensor_mul(ecb[:, 12:16],
                                                   ecb[:, 12:16],
                                                   ecb[:, 4:8]),
                         reads=[], writes=ecw)))
                for ii in range(4):
                    It = 4 * gI + ii
                    def zpiece(ii=ii, It=It):
                        p.op(V, (lambda: lambda v: v.scalar_tensor_tensor(
                            zsb[:, 256 * ii:256 * ii + 256],
                            epb[:, 256 * ii:256 * ii + 256],
                            ecb[:, 12 + ii:13 + ii],
                            g1[:, 256 * It:256 * It + 256],
                            ALU.mult, ALU.mult))(),
                            reads=epw + ecw + [b_g1], writes=zsw)
                        for hf in range(2):
                            tslot = 2 * ii + hf
                            def mk_tp(hf=hf, tslot=tslot, ii=ii):
                                def go():
                                    p.op(T, (lambda: lambda t: t.transpose(
                                        ps_t[:, 128 * tslot:128 * tslot + 128],
                                        zsb[:, 256 * ii + 128 * hf:
                                             256 * ii + 128 * hf + 128],
                                        idb[:, :]))(),
                                        reads=zsw + [b_tabs], writes=[b_psT])
                                return go
                            tp_pending.append(mk_tp())
                        if ii == 3:
                            def mk_ev():
                                c0 = (2 * h) % 8
                                dst = zdst.rearrange(
                                    "p (c b t) -> p c b t", c=8, b=16)[
                                    :, c0:c0 + 2, 4 * gI:4 * gI + 4, :]
                                src = ps_t[:, 0:1024].rearrange(
                                    "p (b c t) -> p c b t", b=4, c=2)
                                def go():
                                    p.op(SC, (lambda: lambda s: s.copy(
                                        dst, src))(),
                                        reads=[b_psT],
                                        writes=[b_z[2 * h], b_z[2 * h + 1]])
                                return go
                            tp_pending.append(mk_ev())
                    pieces.append(zpiece)
                return pieces
            epi_queue.extend(mk_epi())
            sprinkle(1)
        if debug and h == 1:
            tap("xn1", xn[:, :], b_xng)
            tap("wqk", wqk[:, :], [b_wqk])
            tap("wvg", wvg[:, :], [b_wvg])
            tap("abt", ab[:, :], [b_ab])
            tap("qd", qd[:, :], b_qd4)
            tap("kd", kd[:, :], b_kd4)
            tap("vr", vr[:, :], [b_vr])
            tap("g1", g1[:, :], [b_g1])
            tap("ep", ep[:, :], [b_ep])
            tap("ec", ec[:, :], [b_ec])
            tap("zst", zst[:, :], [b_zst])
            tap("z1h0", z1[:, :], [b_z[0], b_z[1]])
        if stop_stage == f"h{h}":
            return _finish()

    while pending:
        pending.pop(0)()

    for fn in epi_queue:
        fn()
    del epi_queue[:]
    for fn in tp_pending:
        fn()
    del tp_pending[:]
    tap("qx8", qx8[:, :], [b_qx8])
    tap("qr8", qr8[:, :], [b_qr8])
    tap("z1", z1[:, :], b_z[:8])
    tap("z2", z2[:, :], b_z[8:])

    for c in range(16):
        p.dma(G, wo_b[:, 1024 * c:1024 * c + 1024],
              d_wo[128 * c:128 * c + 128, :],
              writes=(b_xng + [b_wo] if c == 0 else [b_wo]))
    def emit_rs_bcast():
        for c in range(NC):
            p.op(T, (lambda c=c: lambda t: t.matmul(
                ps_x[0:1, 0:128], rscol[:, c:c + 1], idf[:, :],
                start=True, stop=True))(),
                reads=[b_rsc, b_tabs], writes=[b_psX])
            p.op(SC, lambda s2: s2.copy(row[:, 0:128], ps_x[0:1, 0:128]),
                 reads=[b_psX], writes=[b_row])
            p.op(T, lambda t: t.matmul(ps_x[:, 0:128], onr[:, :],
                                       row[:, 0:128], start=True, stop=True),
                 reads=[b_row, b_tabs], writes=[b_psX])
            p.op(SC, (lambda c=c: lambda s2: s2.copy(
                rsb2[:, 128 * c:128 * c + 128], ps_x[:, 0:128]))(),
                reads=[b_psX], writes=[b_bc])
    emit_rs_bcast()
    for c in range(NC):
        p.op(V, (lambda c=c: lambda v: v.tensor_scalar(
            qrb[:, 1024 * c:1024 * c + 1024],
            qr8[:, 1024 * c:1024 * c + 1024], -128.0, None, ALU.add))(),
            reads=[b_qr8],
            writes=([b_vr, b_g1, b_qrb] if c == 0 else [b_qrb]))

    b_fin = [Buf("fin0"), Buf("fin1")]
    b_fot = [Buf("fot0"), Buf("fot1")]
    for i in range(NT):
        sl3 = i % 2
        go = 1024 * sl3
        if sl3 == 0:
            xtile = xrl[:, 0:1024]
            bx = [b_xs[0], b_xs[1]]
        else:
            xtile = ufuk[:, 0:1024]
            bx = [b_uf[0], b_uk[0]]
        p.dma(SY, xtile, d_xtm[128 * i:128 * i + 128, :], writes=bx)
        def em_womm(i=i, go=go):
            def f(t):
                ins = None
                for c in range(16):
                    ztens = z1 if c < 8 else z2
                    zc = (c % 8) * S + 128 * i
                    for hf in range(2):
                        ins = t.matmul(
                            ps_g[:, go + 512 * hf:go + 512 * hf + 512],
                            ztens[:, zc:zc + 128],
                            wo_b[:, 1024 * c + 512 * hf:1024 * c + 512 * hf + 512],
                            start=(c == 0), stop=(c == 15))
                return ins
            return f
        bqx = b_qxb[sl3]
        qxv = qxb[:, 1024 * sl3:1024 * sl3 + 1024].rearrange(
            "p (c f) -> p c f", c=8)
        qx8v = qx8.rearrange("p (c t) -> p c t", c=8)[:, :, 128 * i:128 * i + 128]
        p.op(SC, (lambda qxv=qxv, qx8v=qx8v: lambda s2: s2.activation(
            qxv, qx8v, AF.Identity, bias=cc[:, 10:11]))(),
            reads=[b_qx8, b_cc],
            writes=([b_qsb[0], b_qsb[1], b_rot[0], b_rot[1], bqx]
                    if i < 2 else [bqx]))
        for hf, (psf, bpsf) in enumerate(((ps_a, b_psA), (ps_b, b_psB))):
            def em_fmm(i=i, sl3=sl3, hf=hf, psf=psf):
                def f(t):
                    ins = None
                    for c in range(NC):
                        ins = t.matmul(
                            psf[:, 0:512],
                            qxb[:, 1024 * sl3 + 128 * c:1024 * sl3 + 128 * c + 128],
                            qrb[:, 1024 * c + 512 * hf:1024 * c + 512 * hf + 512],
                            start=(c == 0), stop=(c == NC - 1))
                    return ins
                return f
            p.op(T, em_fmm(), reads=[bqx, b_qrb], writes=[bpsf])
        bg2 = b_g2[sl3]
        for hf, (psf, bpsf) in enumerate(((ps_a, b_psA), (ps_b, b_psB))):
            p.op(V, (lambda hf=hf, psf=psf: lambda v: v.tensor_mul(
                fin[:, 512 * hf:512 * hf + 512], psf[:, 0:512],
                rsb2[:, 512 * hf:512 * hf + 512]))(),
                reads=[bpsf, b_bc],
                writes=([b_ab, b_fin[hf]] if i == 0 else [b_fin[hf]]))
            p.op(SC, (lambda hf=hf, sl3=sl3, i=i: lambda s: s.activation(
                g2t[:, 1024 * sl3 + 512 * hf:1024 * sl3 + 512 * hf + 512],
                fin[:, 512 * hf:512 * hf + 512], AF.Silu,
                scale=st2[:, 32 + i:33 + i]))(),
                reads=[b_fin[hf], b_st2],
                writes=(b_qd4 + b_kd4 + [bg2] if i < 2 else [bg2]))
        def em_womm_hf(i=i, go=go):
            def f(hf):
                def g(t):
                    ins = None
                    for c in range(16):
                        ztens = z1 if c < 8 else z2
                        zc = (c % 8) * S + 128 * i
                        ins = t.matmul(
                            ps_g[:, go + 512 * hf:go + 512 * hf + 512],
                            ztens[:, zc:zc + 128],
                            wo_b[:, 1024 * c + 512 * hf:
                                 1024 * c + 512 * hf + 512],
                            start=(c == 0), stop=(c == 15))
                    return ins
                return g
            return f
        _wo = em_womm_hf()
        for hf in range(2):
            p.op(T, _wo(hf), reads=[b_wo] + b_z, writes=[b_psG])
            p.op(V, (lambda hf=hf, go=go, sl3=sl3: lambda v: v.tensor_mul(
                fin[:, 512 * hf:512 * hf + 512],
                ps_g[:, go + 512 * hf:go + 512 * hf + 512],
                g2t[:, 1024 * sl3 + 512 * hf:1024 * sl3 + 512 * hf + 512]))(),
                reads=[b_psG, bg2], writes=[b_fin[hf]])
            p.op(G, (lambda hf=hf, sl3=sl3, xtile=xtile: lambda g: g.tensor_add(
                fot[:, 512 * hf:512 * hf + 512],
                fin[:, 512 * hf:512 * hf + 512],
                xtile[:, 512 * hf:512 * hf + 512]))(),
                reads=[b_fin[hf]] + bx,
                writes=([b_ab, b_fot[hf]] if i == 0 else [b_fot[hf]]))
            p.dma(SY, d_out[128 * i:128 * i + 128, 512 * hf:512 * hf + 512],
                  fot[:, 512 * hf:512 * hf + 512], reads=[b_fot[hf]])

    p.raw(G, lambda g: g.nop(), waits=p.dma_total_events())

    with nc.Block() as block:
        p.build(block)
    return nc


_NC_CACHE = None


def kernel(L0_attn_x, L0_attn_wq, L0_attn_wk, L0_attn_wv, L0_attn_wg,
           L0_attn_wo, L0_ffn_rhs):
    global _NC_CACHE
    tabs = _host_tables()
    if _NC_CACHE is None:
        _NC_CACHE = build_nc()
    nc = _NC_CACHE
    x = np.ascontiguousarray(np.asarray(L0_attn_x, dtype=np.float32))
    wq = np.ascontiguousarray(np.asarray(L0_attn_wq, dtype=np.float32))
    wk = np.ascontiguousarray(np.asarray(L0_attn_wk, dtype=np.float32))
    wv = np.ascontiguousarray(np.asarray(L0_attn_wv, dtype=np.float32))
    wg = np.ascontiguousarray(np.asarray(L0_attn_wg, dtype=np.float32))
    wo = np.ascontiguousarray(np.asarray(L0_attn_wo, dtype=np.float32))
    rhs = np.ascontiguousarray(np.asarray(L0_ffn_rhs, dtype=np.float32))
    rhsT = np.ascontiguousarray(rhs.T)
    in_maps = []
    for c in range(NCORES):
        xc = np.ascontiguousarray(x[c])
        in_maps.append(dict(
            x_tm=xc, x_fm=np.ascontiguousarray(xc.T),
            wq=wq, wk=wk, wv=wv, wg=wg, wo=wo, rhs=rhs, rhsT=rhsT,
            cosfm=tabs["cosfm"], sinfm=tabs["sinfm"],
            aba=tabs["aba"], abb=tabs["abb"], Pt=tabs["Pt"],
            tri=tabs["tri"], ident=tabs["ident"]))
    res = run_bass_kernel_spmd(nc, in_maps, core_ids=list(range(NCORES)))
    return np.stack([res.results[c]["out"] for c in range(NCORES)])


# version 35
# speedup vs baseline: 1.0011x; 1.0011x over previous
import numpy as np
import ml_dtypes
import concourse.bass as bass
import concourse.mybir as mybir
from concourse.bass_utils import run_bass_kernel_spmd

F32 = mybir.dt.float32
BF16 = mybir.dt.bfloat16
F16 = mybir.dt.float16
U8 = mybir.dt.uint8
AF = mybir.ActivationFunctionType
ALU = mybir.AluOpType
AX = mybir.AxisListType
BFNP = ml_dtypes.bfloat16

NCORES = 8
S = 2048
D = 1024
H = 8
NT = 16
NC = 8
SBIG = 1.0e8
FOLDS = [128, 64, 32, 16, 8, 4, 2, 1]
VRS = 272

ENGINES = ("tensor", "vector", "scalar", "gpsimd", "sync")
V, SC, G, T, SY = "vector", "scalar", "gpsimd", "tensor", "sync"


class Event:
    __slots__ = ("sem", "value")
    def __init__(self, sem, value):
        self.sem = sem
        self.value = value


class _Sem:
    __slots__ = ("handle", "count", "name")
    def __init__(self, handle, name):
        self.handle = handle
        self.count = 0
        self.name = name


class Buf:
    __slots__ = ("w", "rs", "name")
    def __init__(self, name=""):
        self.w = None
        self.rs = []
        self.name = name


NDMALANES = 6


class Prog:
    def __init__(self, nc):
        self.nc = nc
        self.sems = {e: _Sem(nc.alloc_semaphore(f"p_{e}"), e) for e in ENGINES}
        self.dlanes = {e: [_Sem(nc.alloc_semaphore(f"d_{e}{i}"), f"d_{e}{i}")
                          for i in range(NDMALANES)]
                       for e in (SY, G, SC)}
        self.drr = {e: 0 for e in (SY, G, SC)}
        self.ops = {e: [] for e in ENGINES}
        self.n_ops = 0

    def _gather(self, reads, writes, waits):
        ws = list(waits)
        for b in reads:
            if b.w is not None:
                ws.append(b.w)
        for b in writes:
            if b.w is not None:
                ws.append(b.w)
            ws.extend(b.rs)
        return ws

    def _add(self, eng, emit, ws, inc):
        s = self.sems[eng]
        ev = None
        if inc:
            s.count += inc
            ev = Event(s, s.count)
        self.ops[eng].append((tuple(w for w in ws if w is not None), emit, ev, inc))
        self.n_ops += 1
        return ev

    def _commit(self, ev, reads, writes):
        for b in reads:
            b.rs.append(ev)
        for b in writes:
            b.w = ev
            b.rs = []

    def op(self, eng, emit, reads=(), writes=(), waits=()):
        ev = self._add(eng, emit, self._gather(reads, writes, waits), 1)
        self._commit(ev, reads, writes)
        return ev

    def dma(self, eng, out, in_, reads=(), writes=(), waits=(), tr=False):
        lane = self.dlanes[eng][self.drr[eng]]
        self.drr[eng] = (self.drr[eng] + 1) % NDMALANES
        ws = self._gather(reads, writes, waits)
        if lane.count:
            ws.append(Event(lane, lane.count))
        if tr:
            def emit(e, out=out, in_=in_):
                return e.dma_start_transpose(out, in_)
        else:
            def emit(e, out=out, in_=in_):
                return e.dma_start(out=out, in_=in_)
        lane.count += 16
        ev = Event(lane, lane.count)
        self.ops[eng].append((tuple(w for w in ws if w is not None),
                              emit, ev, 16))
        self.n_ops += 1
        self._commit(ev, reads, writes)
        return ev

    def dma_total_events(self):
        evs = []
        for e in (SY, G, SC):
            for lane in self.dlanes[e]:
                if lane.count:
                    evs.append(Event(lane, lane.count))
        return evs

    def raw(self, eng, emit, waits=()):
        return self._add(eng, emit, list(waits), 0)

    def build(self, block):
        for eng_name in ENGINES:
            if not self.ops[eng_name]:
                continue
            def run(eng, eng_name=eng_name):
                seen = {}
                my = self.sems[eng_name]
                for waits, emit, ev, inc in self.ops[eng_name]:
                    need = {}
                    for w in waits:
                        if w.sem is my and eng_name in ("sync", "tensor"):
                            continue
                        k = id(w.sem)
                        if w.value > need.get(k, (None, -1))[1]:
                            need[k] = (w.sem, w.value)
                    for sem, val in need.values():
                        if seen.get(id(sem), -1) >= val:
                            continue
                        eng.wait_ge(sem.handle, val)
                        seen[id(sem)] = val
                    ins = emit(eng)
                    if ev is not None:
                        ins.then_inc(ev.sem.handle, inc)
            getattr(block, eng_name)(run)


_TABLES = None

def _host_tables():
    global _TABLES
    if _TABLES is not None:
        return _TABLES
    m_idx = np.arange(64, dtype=np.float64)
    theta = 10000.0 ** (-m_idx / 63.0)
    t_idx = np.arange(S, dtype=np.float64)
    ang = t_idx[:, None] * theta[None, :]
    cos_fm = np.ascontiguousarray(np.repeat(np.cos(ang), 2, axis=1).T.astype(np.float32))
    sin_fm = np.ascontiguousarray(np.repeat(np.sin(ang), 2, axis=1).T.astype(np.float32))
    d_h = 1.0 - 2.0 ** (-5.0 - np.arange(H, dtype=np.float64))
    dpow = d_h[:, None] ** t_idx[None, :]
    msum = (1.0 - d_h[:, None] ** (t_idx[None, :] + 1.0)) / (1.0 - d_h[:, None])
    a_tab = (dpow / np.sqrt(msum)).astype(np.float32)
    b_tab = ((1.0 / dpow) * (128.0 ** -0.5)).astype(np.float32)
    ab_a = np.ascontiguousarray(np.broadcast_to(a_tab[:, None, :], (H, 128, S)))
    ab_b = np.ascontiguousarray(np.broadcast_to(b_tab[:, None, :], (H, 128, S)))
    P = np.zeros((128, 128), dtype=np.float32)
    for m2 in range(64):
        P[2 * m2, 2 * m2 + 1] = -1.0
        P[2 * m2 + 1, 2 * m2] = 1.0
    _TABLES = dict(
        cosfm=cos_fm, sinfm=sin_fm,
        aba=ab_a.astype(np.float32), abb=ab_b.astype(np.float32),
        Pt=np.ascontiguousarray(P.T),
        tri=np.ascontiguousarray(
            (np.arange(128)[:, None] <= np.arange(128)[None, :]).astype(np.float32)),
        ident=np.eye(128, dtype=np.float32))
    return _TABLES


def build_nc(stop_stage=None, debug=False):
    nc = bass.Bass("TRN2", target_bir_lowering=False)

    d_xtm = nc.dram_tensor("x_tm", [S, D], F32, kind="ExternalInput").ap()
    d_xfm = nc.dram_tensor("x_fm", [D, S], F32, kind="ExternalInput").ap()
    d_wq = nc.dram_tensor("wq", [D, D], F32, kind="ExternalInput").ap()
    d_wk = nc.dram_tensor("wk", [D, D], F32, kind="ExternalInput").ap()
    d_wv = nc.dram_tensor("wv", [D, 2 * D], F32, kind="ExternalInput").ap()
    d_wg = nc.dram_tensor("wg", [D, 2 * D], F32, kind="ExternalInput").ap()
    d_wo = nc.dram_tensor("wo", [2 * D, D], F32, kind="ExternalInput").ap()
    d_rhsT = nc.dram_tensor("rhsT", [D, D], F32, kind="ExternalInput").ap()
    d_rhs = nc.dram_tensor("rhs", [D, D], F32, kind="ExternalInput").ap()
    d_cos = nc.dram_tensor("cosfm", [128, S], F32, kind="ExternalInput").ap()
    d_sin = nc.dram_tensor("sinfm", [128, S], F32, kind="ExternalInput").ap()
    d_aba = nc.dram_tensor("aba", [H, 128, S], F32, kind="ExternalInput").ap()
    d_abb = nc.dram_tensor("abb", [H, 128, S], F32, kind="ExternalInput").ap()
    d_Pt = nc.dram_tensor("Pt", [128, 128], F32, kind="ExternalInput").ap()
    d_tri = nc.dram_tensor("tri", [128, 128], F32, kind="ExternalInput").ap()
    d_id = nc.dram_tensor("ident", [128, 128], F32, kind="ExternalInput").ap()
    d_out = nc.dram_tensor("out", [S, D], F32, kind="ExternalOutput").ap()

    _off = [0]
    def region(nbytes, align=64):
        o = (_off[0] + align - 1) // align * align
        _off[0] = o + nbytes
        return o
    _arena = [None]
    def view(off, cols, dtype, rows=128):
        esz = {F32: 4, BF16: 2, F16: 2, U8: 1}[dtype]
        return _arena[0][0:rows, off:off + cols * esz].bitcast(dtype)

    o_xn = region(32768)
    o_qx = region(16384)
    o_z1 = region(32768)
    o_z2 = region(32768)
    o_cs = region(8192)
    o_bc = region(4096)
    o_qr = region(8192)
    o_ab = region(8192)
    o_qd = region(4096)
    o_kd = region(4096)
    o_qs = region(4096)
    o_vr = region(2 * 16 * VRS)
    o_g1 = region(8192)
    o_sc = region(2048)
    o_wqk = region(4096)
    o_wvg = region(8192)
    o_xs = region(4096)
    o_uf = region(2048)
    o_uk = region(2048)
    o_ktm = region(4096)
    o_ts = region(2048)
    o_rt = region(2048)
    o_row = region(2048)
    o_stc = region(512)
    o_st2 = region(512)
    o_st3 = region(256)
    o_rsc = region(64)
    o_cc = region(64)
    o_pt = region(256)
    o_tri = region(256)
    o_idb = region(256)
    o_idf = region(512)
    o_or = region(512)
    o_ep = region(2048)
    o_ec = region(128)
    o_zs = region(2048)
    ARENA = (_off[0] + 63) // 64 * 64
    avail = nc.sbuf_bytes_remaining
    assert ARENA <= avail, (ARENA, avail)
    _arena[0] = nc.alloc_sbuf_tensor("arena", [128, ARENA], U8).ap()

    xn = view(o_xn, NC * S, BF16)
    wo_b = view(o_xn, 16 * D, BF16)
    qx8 = view(o_qx, NC * S, U8)
    z1 = view(o_z1, 8 * S, BF16)
    z2 = view(o_z2, 8 * S, BF16)
    cs = view(o_cs, 2 * S, BF16)
    bc = view(o_bc, 1024, F32)
    qr8 = view(o_qr, 8192, U8)
    ab = view(o_ab, 2 * S, BF16)
    qd = view(o_qd, S, BF16)
    kd = view(o_kd, S, BF16)
    qsb = view(o_qs, S, BF16)
    vr = view(o_vr, 2 * 16 * VRS // 2, BF16)
    g1 = view(o_g1, 16 * 256, BF16)
    scsb = view(o_sc, 1024, BF16)
    wqk = view(o_wqk, 2048, BF16)
    wvg = view(o_wvg, 4096, BF16)
    xs = view(o_xs, 1024, F32)
    uf = view(o_uf, 512, F32)
    uk = view(o_uk, 512, F32)
    ufuk = view(o_uf, 1024, F32)
    b0h = view(o_uf, 1024, F16)
    bwh = view(o_uk, 1024, F16)
    thx = [view(o_xs, 1024, F16), view(o_bc, 1024, F16)]
    xqx = [view(o_xs, 1024, F32), view(o_bc, 1024, F32)]
    rowq = view(o_row, 1024, BF16)
    qsf = view(o_qs, 1024, F32)
    ktm = view(o_ktm, 16 * 128, BF16)
    tst = view(o_ts, 1024, BF16)
    rtt = view(o_rt, 1024, BF16)
    row = view(o_row, 512, F32, rows=1)
    stc = view(o_stc, 64, F32)
    st2 = view(o_st2, 64, F32)
    st3 = view(o_st3, 32, F32)
    rscol = view(o_rsc, 16, F32)
    cc = view(o_cc, 16, F32)
    ptb = view(o_pt, 128, BF16)
    trib = view(o_tri, 128, BF16)
    idb = view(o_idb, 128, BF16)
    idf = view(o_idf, 128, F32)
    onr = view(o_or, 128, F32, rows=1)
    ep = view(o_ep, 1024, BF16)
    ec = view(o_ec, 32, F32)
    zst = view(o_zs, 1024, BF16)
    qrb = view(o_vr, 8192, BF16)
    qxb = view(o_qs, S, BF16)
    g2t = view(o_qd, 2048, BF16)
    rsb2 = view(o_bc, 1024, F32)
    xrl = view(o_xs, 1024, F32)

    fin = view(o_ab, 1024, F32)
    fot = view(o_ab + 4096, 1024, F32)

    ps_a = nc.alloc_psum_tensor("psA", [128, 512], F32).ap()
    ps_b = nc.alloc_psum_tensor("psB", [128, 512], F32).ap()
    ps_g = nc.alloc_psum_tensor("psG", [128, 2048], F32).ap()
    ps_x = nc.alloc_psum_tensor("psX", [128, 512], F32).ap()
    ps_t = nc.alloc_psum_tensor("psT", [128, 1024], BF16).ap()

    p = Prog(nc)

    _taps = {}
    def tap(name, ap, reads):
        if not debug or (isinstance(debug, (set, frozenset, list, tuple))
                         and name not in debug):
            return
        shp = list(ap.shape)
        dt_ = ap.dtype
        o_ = nc.dram_tensor("tap_" + name, shp, dt_, kind="ExternalOutput").ap()
        p.dma(SY, o_, ap, reads=list(reads))

    b_xng = [Buf("xn0"), Buf("xn1"), Buf("xn2"), Buf("xn3")]
    b_qx8 = Buf("qx8")
    b_z = [Buf(f"z{i}") for i in range(16)]
    b_cs = Buf("cs"); b_bc = Buf("bc"); b_qr8 = Buf("qr8"); b_ab = Buf("ab")
    b_qd4 = [Buf(f"qd{t}") for t in range(4)]
    b_kd4 = [Buf(f"kd{t}") for t in range(4)]
    b_ktm = Buf("ktm")
    b_qsb = [Buf("qsb0"), Buf("qsb1")]
    b_rot = [Buf("rot0"), Buf("rot1")]
    b_vr = Buf("vr"); b_g1 = Buf("g1")
    b_scsb = [Buf("sc0"), Buf("sc1")]
    b_wqk = Buf("wqk"); b_wvg = Buf("wvg")
    b_xs = [Buf("xs0"), Buf("xs1")]
    b_uf = [Buf("uf0"), Buf("uf1")]
    b_uk = [Buf("uk0"), Buf("uk1")]
    b_tst = [Buf("ts0"), Buf("ts1")]
    b_rtt = [Buf("rt0"), Buf("rt1")]
    b_row = Buf("row")
    b_stc = Buf("stc"); b_st2 = Buf("st2"); b_st3 = Buf("st3")
    b_rsc = Buf("rscol")
    b_cc = Buf("cc"); b_tabs = Buf("tabs")
    b_ep = Buf("ep"); b_ec = Buf("ec"); b_ec2 = Buf("ec2"); b_zst = Buf("zst")
    b_psA = Buf("psA"); b_psB = Buf("psB"); b_psG = Buf("psG")
    b_psX = Buf("psX"); b_psT = Buf("psT")
    b_wo = Buf("wo_b"); b_qrb = Buf("qrb")
    b_qxb = [Buf("qxb0"), Buf("qxb1")]
    b_g2 = [Buf("g2a"), Buf("g2b")]

    def em_consts(v):
        v.memset(cc[:, 0:1], SBIG)
        v.memset(cc[:, 9:10], 128.5)
        v.memset(cc[:, 10:11], -128.0)
        ins = None
        for i, b in enumerate(FOLDS):
            ins = v.memset(cc[:, 1 + i:2 + i], -SBIG * float(b))
        return ins
    p.op(V, em_consts, writes=[b_cc])
    p.op(V, lambda v: v.memset(onr[:, :], 1.0), writes=[b_tabs])
    def em_vrones(v):
        ins = None
        for i in range(16):
            ins = v.memset(vr[:, VRS * i + 256:VRS * i + 257], 1.0)
            ins = v.memset(vr[:, VRS * i + 257:VRS * i + 260], 0.0)
        return ins
    p.op(V, em_vrones, writes=[b_vr])
    p.dma(G, cs[:, 0:S], d_cos, writes=[b_cs])
    p.dma(G, cs[:, S:2 * S], d_sin, writes=[b_cs])
    p.dma(G, ptb[:, :], d_Pt, writes=[b_tabs])
    p.dma(G, trib[:, :], d_tri, writes=[b_tabs])
    p.dma(G, idb[:, :], d_id, writes=[b_tabs])
    p.dma(SY, idf[:, :], d_id, writes=[b_tabs])

    _xpb = [b_xs[0], b_xs[1]]
    _bcb = [b_bc]
    def _xpw(slot):
        return _xpb if slot == 0 else _bcb
    def _stage(slot):
        return (tst, b_tst[0]) if slot == 0 else (rowq, b_row)

    _p0bufs = [
        (xs[:, 0:1024], [b_xs[0], b_xs[1]]),
        (xqx[1], [b_bc]),
        (ufuk[:, 0:1024], [b_uf[0], b_uk[0]]),
        (qsf, [b_qsb[0], b_qsb[1], b_rot[0], b_rot[1]]),
    ]

    def _statmath(tg):
        a = 4 * tg
        m_ = st2[:, a:a + 4]
        rs_ = st2[:, 16 + a:16 + a + 4]
        ls_ = st2[:, 32 + a:32 + a + 4]
        sc_ = st2[:, 48 + a:48 + a + 4]
        e_ = ec[:, a:a + 4]
        p.op(V, lambda v: v.tensor_scalar(m_, stc[:, a:a + 4], 1.0 / D, None,
                                          ALU.mult),
             reads=[b_stc], writes=[b_st2])
        p.op(V, lambda v: v.tensor_mul(sc_, m_, m_),
             reads=[b_st2], writes=[b_st2])
        p.op(V, lambda v: v.scalar_tensor_tensor(
            sc_, stc[:, 16 + a:16 + a + 4], 1.0 / D, sc_,
            ALU.mult, ALU.subtract), reads=[b_stc, b_st2], writes=[b_st2])
        p.op(V, lambda v: v.tensor_scalar(sc_, sc_, 1e-5, None, ALU.add),
             reads=[b_st2], writes=[b_st2])
        p.op(V, lambda v: v.reciprocal(rs_, sc_),
             reads=[b_st2], writes=[b_st2])
        p.op(SC, lambda s: s.activation(rs_, rs_, AF.Sqrt),
             reads=[b_st2], writes=[b_st2])
        p.op(V, lambda v: v.tensor_sub(sc_, stc[:, 32 + a:32 + a + 4], m_),
             reads=[b_stc, b_st2], writes=[b_st2])
        p.op(V, lambda v: v.tensor_sub(e_, m_, stc[:, 48 + a:48 + a + 4]),
             reads=[b_stc, b_st2], writes=[b_ec])
        p.op(V, lambda v: v.tensor_max(sc_, sc_, e_),
             reads=[b_ec], writes=[b_st2])
        p.op(V, lambda v: v.tensor_mul(sc_, sc_, rs_),
             reads=[b_st2], writes=[b_st2])
        p.op(V, lambda v: v.tensor_scalar(sc_, sc_, 1e-12, None, ALU.max),
             reads=[b_st2], writes=[b_st2])
        p.op(V, lambda v: v.tensor_scalar(ls_, sc_, 1.0 / 127.0, None,
                                          ALU.mult),
             reads=[b_st2], writes=[b_st2])
        p.op(V, lambda v: v.reciprocal(e_, sc_),
             reads=[b_st2], writes=[b_ec])
        p.op(V, lambda v: v.scalar_tensor_tensor(
            st3[:, a:a + 4], e_, 127.0, rs_, ALU.mult, ALU.mult),
            reads=[b_ec, b_st2], writes=[b_st3])
        p.op(V, lambda v: v.tensor_mul(st3[:, 16 + a:16 + a + 4], m_,
                                       st3[:, a:a + 4]),
             reads=[b_st2, b_st3], writes=[b_st3])
        p.op(V, lambda v: v.tensor_scalar(
            st3[:, 16 + a:16 + a + 4], st3[:, 16 + a:16 + a + 4],
            -128.5, None, ALU.add), reads=[b_st3], writes=[b_st3])
        p.op(V, lambda v: v.tensor_mul(sc_, m_, rs_),
             reads=[b_st2], writes=[b_st2])

    for tg in range(4):
        for k in range(4):
            i = 4 * tg + k
            sl, bsl = _p0bufs[k]
            p.dma(SY, sl, d_xtm[128 * i:128 * i + 128, :], writes=bsl)
            p.op(V, (lambda sl=sl, i=i: lambda v: v.reduce_max(
                stc[:, 32 + i:33 + i], sl, axis=AX.X))(),
                reads=bsl, writes=[b_stc])
            p.op(V, (lambda sl=sl, i=i: lambda v: v.tensor_reduce(
                stc[:, 48 + i:49 + i], sl, axis=AX.X, op=ALU.min))(),
                reads=bsl, writes=[b_stc])
            p.op(V, (lambda sl=sl, i=i: lambda v: v.reduce_sum(
                stc[:, i:i + 1], sl, axis=AX.X))(),
                reads=bsl, writes=[b_stc])
            p.op(SC, (lambda sl=sl: lambda s: s.activation(
                ep[:, 0:1024], sl, AF.Square))(),
                reads=bsl, writes=[b_ep])
            p.op(V, (lambda i=i: lambda v: v.reduce_sum(
                stc[:, 16 + i:17 + i], ep[:, 0:1024], axis=AX.X))(),
                reads=[b_ep], writes=[b_stc])
        _statmath(tg)
        for k in range(4):
            i = 4 * tg + k
            sl, bsl = _p0bufs[k]
            stg, bstg = _stage(k % 2)
            p.op(V, (lambda i=i, sl=sl, stg=stg: lambda v: v.tensor_scalar(
                stg[:, 0:1024], sl,
                st2[:, 16 + i:17 + i], st2[:, 48 + i:49 + i],
                ALU.mult, ALU.subtract))(),
                reads=bsl + [b_st2], writes=[bstg])
            for j in range(8):
                p.op(T, (lambda j=j, stg=stg: lambda t: t.transpose(
                    ps_t[:, 128 * j:128 * j + 128],
                    stg[:, 128 * j:128 * j + 128], idb[:, :]))(),
                    reads=[bstg, b_tabs], writes=[b_psT])
            dst = xn.rearrange("p (c t) -> p c t", c=NC)[
                :, :, 128 * i:128 * i + 128]
            src = ps_t[:, 0:1024].rearrange("p (j c) -> p j c", j=8)
            p.op(SC, (lambda dst=dst, src=src: lambda s: s.copy(dst, src))(),
                 reads=[b_psT], writes=[b_xng[tg]])

    def emit_col2bcast(colap, breads, dst_off, tg):
        for j in range(4):
            i = 4 * tg + j
            p.op(T, (lambda i=i, j=j, colap=colap: lambda t: t.matmul(
                ps_x[0:1, 128 * j:128 * j + 128], colap[:, i:i + 1], idf[:, :],
                start=True, stop=True))(),
                reads=list(breads) + [b_tabs], writes=[b_psX])
        p.op(SC, lambda s2: s2.copy(row[:, 0:512], ps_x[0:1, 0:512]),
             reads=[b_psX], writes=[b_row])
        p.op(T, lambda t: t.matmul(ps_x[:, 0:512], onr[:, :], row[:, :],
                                   start=True, stop=True),
             reads=[b_row, b_tabs], writes=[b_psX])
        p.op(SC, (lambda dst_off=dst_off: lambda s2: s2.copy(
            bc[:, dst_off:dst_off + 512], ps_x[:, 0:512]))(),
            reads=[b_psX], writes=[b_bc])

    def _finish():
        while pending:
            pending.pop(0)()
        for fn in epi_queue:
            fn()
        del epi_queue[:]
        for fn in tp_pending:
            fn()
        del tp_pending[:]
        p.raw(G, lambda g: g.nop(), waits=p.dma_total_events())
        all_sems = [s.handle for s in p.sems.values()]
        for lanes in p.dlanes.values():
            all_sems.extend(l.handle for l in lanes)
        for sh in all_sems:
            nc.gpsimd.sem_clear(sh)
        nc.all_engine_barrier()
        with nc.Block() as block:
            p.build(block)
        return nc

    tap("xn", xn[:, :], b_xng)
    tap("bc", bc[:, :], [b_bc])

    pending = []

    _spg = [True]
    def sprinkle(n):
        if not _spg[0]:
            return
        k = 0
        while pending and k < n:
            pending.pop(0)()
            k += 1

    def emit_fold_chain(ufl, tsl, buf_, bts_, eng=V):
        for bi, bv in enumerate(FOLDS):
            p.op(eng, (lambda bv=bv, ufl=ufl, tsl=tsl: lambda e:
                       e.tensor_scalar(
                tsl, ufl, float(bv), -float(bv), ALU.is_ge, ALU.mult))(),
                reads=[buf_], writes=[bts_])
            p.op(eng, (lambda ufl=ufl, tsl=tsl: lambda e: e.tensor_add(
                ufl, ufl, tsl))(),
                reads=[bts_], writes=[buf_])

    pidx = [0]

    def _fold_chain(slot, buf_, buk_):
        th = thx[slot]
        bx = _xpw(slot)
        first = True
        for bv in FOLDS:
            bsrc = b0h if first else bwh
            p.op(V, (lambda bv=bv, bsrc=bsrc, th=th: lambda v:
                     v.tensor_scalar(
                th[:, 0:1024], bsrc[:, 0:1024], float(bv), -float(bv),
                ALU.is_ge, ALU.mult))(),
                reads=[buf_ if first else buk_], writes=bx)
            p.op(V, (lambda bsrc=bsrc, th=th: lambda v: v.tensor_add(
                bwh[:, 0:1024], bsrc[:, 0:1024], th[:, 0:1024]))(),
                reads=bx + ([buf_] if first else []), writes=[buk_])
            first = False

    def emit_xq_v(i):
        slot = i % 2
        xsl = xqx[slot]
        bx = _xpw(slot)
        buf_, buk_ = b_uf[0], b_uk[0]
        stg, bstg = _stage(slot)
        p.dma(SY, xsl, d_xtm[128 * i:128 * i + 128, :], writes=bx)
        p.op(V, (lambda i=i, xsl=xsl: lambda v: v.tensor_scalar(
            b0h[:, 0:1024], xsl, st3[:, i:i + 1], st3[:, 16 + i:17 + i],
            ALU.mult, ALU.subtract))(),
            reads=bx + [b_st3], writes=[buf_])
        _fold_chain(slot, buf_, buk_)
        p.op(V, (lambda stg=stg: lambda v: v.tensor_sub(
            stg[:, 0:1024], b0h[:, 0:1024], bwh[:, 0:1024]))(),
            reads=[buf_, buk_], writes=[bstg])

    def emit_xq_t(i):
        slot = i % 2
        stg, bstg = _stage(slot)
        for j in range(8):
            p.op(T, (lambda j=j, stg=stg: lambda t: t.transpose(
                ps_t[:, 128 * j:128 * j + 128],
                stg[:, 128 * j:128 * j + 128], idb[:, :]))(),
                reads=[bstg, b_tabs], writes=[b_psT])
        dst = qx8.rearrange("p (c t) -> p c t", c=NC)[
            :, :, 128 * i:128 * i + 128]
        src = ps_t[:, 0:1024].rearrange("p (j c) -> p j c", j=8)
        p.op(SC, (lambda dst=dst, src=src: lambda s: s.copy(dst, src))(),
             reads=[b_psT], writes=[b_qx8])

    def emit_rhsA(c):
        bxs = [b_xs[0], b_xs[1]]
        xsl = xs[:, 0:1024]
        p.dma(SY, xsl, d_rhsT[128 * c:128 * c + 128, :], writes=bxs)
        p.op(V, (lambda c=c, xsl=xsl: lambda v: v.reduce_max(
            ec[:, 24:25], xsl, axis=AX.X, apply_absolute_value=True))(),
            reads=bxs, writes=[b_ec2])
        p.op(V, (lambda c=c: lambda v: v.tensor_scalar(
            rscol[:, c:c + 1], ec[:, 24:25], 1e-12, 1.0 / 127.0,
            ALU.max, ALU.mult))(),
            reads=[b_ec2], writes=[b_rsc])

    def emit_rhsB_v(cf):
        slot = cf % 2
        xsl = xqx[slot]
        bx = _xpw(slot)
        buf_, buk_ = b_uf[0], b_uk[0]
        stg, bstg = _stage(slot)
        p.dma(SY, xsl, d_rhsT[128 * cf:128 * cf + 128, :], writes=bx)
        p.op(V, (lambda cf=cf, xsl=xsl: lambda v: v.tensor_scalar(
            b0h[:, 0:1024], xsl,
            rscol[:, 8 + cf:9 + cf], cc[:, 9:10],
            ALU.mult, ALU.add))(),
            reads=bx + [b_rsc, b_cc], writes=[buf_])
        _fold_chain(slot, buf_, buk_)
        p.op(V, (lambda stg=stg: lambda v: v.tensor_sub(
            stg[:, 0:1024], b0h[:, 0:1024], bwh[:, 0:1024]))(),
            reads=[buf_, buk_], writes=[bstg])

    def emit_rhsB_t(cf):
        slot = cf % 2
        stg, bstg = _stage(slot)
        for j in range(8):
            p.op(T, (lambda j=j, stg=stg: lambda t: t.transpose(
                ps_t[:, 128 * j:128 * j + 128],
                stg[:, 128 * j:128 * j + 128], idb[:, :]))(),
                reads=[bstg, b_tabs], writes=[b_psT])
        dst = qr8.rearrange("p (c f) -> p c f", c=NC)[
            :, :, 128 * cf:128 * cf + 128]
        src = ps_t[:, 0:1024].rearrange("p (j c) -> p j c", j=8)
        p.op(SC, (lambda dst=dst, src=src: lambda s: s.copy(dst, src))(),
             reads=[b_psT], writes=[b_qr8])

    def emit_invrs_bcast():
        p.op(V, lambda v: v.reciprocal(ec[:, 16:24], rscol[:, 0:8]),
             reads=[b_rsc], writes=[b_ec2])
        for c in range(NC):
            p.op(T, (lambda c=c: lambda t: t.matmul(
                ps_x[0:1, 0:128], ec[:, 16 + c:17 + c], idf[:, :],
                start=True, stop=True))(),
                reads=[b_ec2, b_tabs], writes=[b_psX])
            p.op(SC, lambda s2: s2.copy(row[:, 0:128], ps_x[0:1, 0:128]),
                 reads=[b_psX], writes=[b_row])
            p.op(T, lambda t: t.matmul(ps_x[:, 0:128], onr[:, :],
                                       row[:, 0:128], start=True, stop=True),
                 reads=[b_row, b_tabs], writes=[b_psX])
            p.op(SC, (lambda c=c: lambda s2: s2.copy(
                bc[:, 128 * c:128 * c + 128], ps_x[:, 0:128]))(),
                reads=[b_psX], writes=[b_bc])

    for c in range(NC):
        pending.append((lambda c=c: lambda: emit_rhsA(c))())
    pending.append(lambda: p.op(V, lambda v: v.reciprocal(
        rscol[:, 8:16], rscol[:, 0:8]), reads=[b_rsc], writes=[b_rsc]))
    for i in range(NT):
        pending.append((lambda i=i: lambda: emit_xq_v(i))())
        if i > 0:
            pending.append((lambda i=i: lambda: emit_xq_t(i - 1))())
    pending.append(lambda: emit_xq_t(NT - 1))
    for cf in range(NC):
        pending.append((lambda cf=cf: lambda: emit_rhsB_v(cf))())
        if cf > 0:
            pending.append((lambda cf=cf: lambda: emit_rhsB_t(cf - 1))())
    pending.append(lambda: emit_rhsB_t(NC - 1))

    pp = [(ps_b, b_psB), (ps_x, b_psX)]
    ping = [0]
    def nxt():
        ping[0] ^= 1
        return pp[ping[0]]

    def load_head_weights(h):
        dst = wqk[:, 0:1024].rearrange("p (c f) -> p c f", c=8)
        src = d_wq.rearrange("(c p) f -> p c f", p=128)[:, :, 128 * h:128 * h + 128]
        p.dma(G, dst, src, writes=[b_wqk])
        dst = wqk[:, 1024:2048].rearrange("p (c f) -> p c f", c=8)
        src = d_wk.rearrange("(c p) f -> p c f", p=128)[:, :, 128 * h:128 * h + 128]
        p.dma(G, dst, src, writes=[b_wqk])
        dst = wvg[:, 0:2048].rearrange("p (c f) -> p c f", c=8)
        src = d_wv.rearrange("(c p) f -> p c f", p=128)[:, :, 256 * h:256 * h + 256]
        p.dma(G, dst, src, writes=[b_wvg])
        dst = wvg[:, 2048:4096].rearrange("p (c f) -> p c f", c=8)
        src = d_wg.rearrange("(c p) f -> p c f", p=128)[:, :, 256 * h:256 * h + 256]
        p.dma(G, dst, src, writes=[b_wvg])
        p.dma(G, ab[:, 0:S], d_aba[h], writes=[b_ab])
        p.dma(G, ab[:, S:2 * S], d_abb[h], writes=[b_ab])

    load_head_weights(0)
    tp_pending = []
    epi_queue = []

    if stop_stage == "p0":
        return _finish()

    for h in range(H):
        _spg[0] = h >= 2
        for fn in epi_queue:
            fn()
        del epi_queue[:]
        for i2 in range(0, NT, 2):
            ps, bps = nxt()
            def em_vmm(i2=i2, ps=ps):
                def f(t):
                    ins = None
                    for k in range(2):
                        i = i2 + k
                        for c in range(NC):
                            ins = t.matmul(
                                ps[:, 256 * k:256 * k + 256],
                                xn[:, S * c + 128 * i:S * c + 128 * i + 128],
                                wvg[:, 256 * c:256 * c + 256],
                                start=(c == 0), stop=(c == NC - 1))
                    return ins
                return f
            p.op(T, em_vmm(), reads=[b_xng[i2 // 4], b_wvg], writes=[bps])
            p.op(SC, (lambda i2=i2, ps=ps: lambda s: s.copy(
                vr.rearrange("p (s t) -> p s t", s=16)[:, i2:i2 + 2, 0:256],
                ps[:, 0:512].rearrange("p (s t) -> p s t", s=2)))(),
                reads=[bps], writes=[b_vr])
            if i2 == 2 and tp_pending:
                for fn in tp_pending:
                    fn()
                del tp_pending[:]
            if i2 % 4 == 2:
                sprinkle(1)
        for i2 in range(0, NT, 2):
            ps, bps = nxt()
            def em_gmm(i2=i2, ps=ps):
                def f(t):
                    ins = None
                    for k in range(2):
                        i = i2 + k
                        for c in range(NC):
                            ins = t.matmul(
                                ps[:, 256 * k:256 * k + 256],
                                xn[:, S * c + 128 * i:S * c + 128 * i + 128],
                                wvg[:, 2048 + 256 * c:2048 + 256 * c + 256],
                                start=(c == 0), stop=(c == NC - 1))
                    return ins
                return f
            p.op(T, em_gmm(), reads=[b_xng[i2 // 4], b_wvg], writes=[bps])
            p.op(SC, (lambda i2=i2, ps=ps: lambda s: s.activation(
                g1[:, 256 * i2:256 * i2 + 512], ps[:, 0:512], AF.Silu))(),
                reads=[bps], writes=[b_g1])

        for qk in range(2):
            wofs = 1024 * qk
            dstd = qd if qk == 0 else kd
            bdd4 = b_qd4 if qk == 0 else b_kd4
            abo = 0 if qk == 0 else S
            for tg in range(4):
                ps, bps = nxt()
                sslot = tg % 2
                qso = 1024 * sslot
                roto = 1024 * sslot + 512
                bq, br = b_qsb[sslot], b_rot[sslot]
                def em_qmm(tg=tg, ps=ps, wofs=wofs):
                    def f(t):
                        ins = None
                        for c in range(NC):
                            ins = t.matmul(
                                ps[:, 0:512],
                                wqk[:, wofs + 128 * c:wofs + 128 * c + 128],
                                xn[:, S * c + 512 * tg:S * c + 512 * tg + 512],
                                start=(c == 0), stop=(c == NC - 1))
                        return ins
                    return f
                p.op(T, em_qmm(), reads=[b_xng[tg], b_wqk], writes=[bps])
                p.op(SC, (lambda qso=qso, ps=ps: lambda s: s.copy(
                    qsb[:, qso:qso + 512], ps[:, 0:512]))(),
                    reads=[bps], writes=[bq])
                p.op(T, (lambda qso=qso: lambda t: t.matmul(
                    ps_x[:, 0:512], ptb[:, :], qsb[:, qso:qso + 512],
                    start=True, stop=True))(),
                    reads=[bq, b_tabs], writes=[b_psX])
                to = 512 * tg
                rts = rtt[:, 512 * sslot:512 * sslot + 512]
                brt = b_rtt[sslot]
                p.op(V, (lambda qso=qso, to=to, rts=rts: lambda v: v.tensor_mul(
                    rts, qsb[:, qso:qso + 512], cs[:, to:to + 512]))(),
                    reads=[bq, b_cs], writes=[brt])
                p.op(V, (lambda to=to, qso=qso: lambda v: v.tensor_mul(
                    qsb[:, qso:qso + 512], ps_x[:, 0:512],
                    cs[:, S + to:S + to + 512]))(),
                    reads=[b_psX, b_cs], writes=[bq])
                p.op(V, (lambda rts=rts, qso=qso: lambda v: v.tensor_add(
                    rts, rts, qsb[:, qso:qso + 512]))(),
                    reads=[bq], writes=[brt])
                p.op(V, (lambda dstd=dstd, to=to, abo=abo, rts=rts: lambda v:
                         v.tensor_mul(
                    dstd[:, to:to + 512], rts, ab[:, abo + to:abo + to + 512]))(),
                    reads=[brt, b_ab], writes=[bdd4[tg]])
            sprinkle(1)

        for b8 in range(2):
            for k8 in range(8):
                i = 8 * b8 + k8
                p.op(T, (lambda i=i, k8=k8: lambda t: t.transpose(
                    ps_t[:, 128 * k8:128 * k8 + 128],
                    kd[:, 128 * i:128 * i + 128],
                    idb[:, :]))(),
                    reads=[b_kd4[i // 4], b_tabs], writes=[b_psT])
            p.op(SC, (lambda b8=b8: lambda s: s.copy(
                ktm[:, 1024 * b8:1024 * b8 + 1024],
                ps_t[:, 0:1024]))(),
                reads=[b_psT], writes=[b_ktm])

        if h + 1 < H:
            load_head_weights(h + 1)

        zdst = z1 if h < 4 else z2
        for gI in range(4):
            qbase = 512 * gI
            if gI > 0:
                for j in range(4 * (gI - 1), 4 * gI):
                    p.op(T, (lambda j=j: lambda t: t.matmul(
                        ps_a[:, 0:257],
                        ktm[:, 128 * j:128 * j + 128],
                        vr[:, VRS * j:VRS * j + 257],
                        start=(j == 0), stop=(j % 4 == 3),
                        skip_group_check=True))(),
                        reads=[b_ktm, b_vr], writes=[b_psA])
                ssl = (gI - 1) % 2
                st_ap = qsb[:, 512 * ssl:512 * ssl + 257]
                bst = b_qsb[0] if ssl == 0 else b_rot[0]
                p.op(SC, (lambda st_ap=st_ap: lambda s: s.copy(
                    st_ap, ps_a[:, 0:257]))(),
                    reads=[b_psA], writes=[bst])
                def mk_outmm(qbase=qbase, st_ap=st_ap, bst=bst):
                    def go():
                        for ii in range(4):
                            p.op(T, (lambda ii=ii: lambda t: t.matmul(
                                ps_g[:, 512 * ii:512 * ii + 257],
                                qd[:, qbase + 128 * ii:qbase + 128 * ii + 128],
                                st_ap, start=True, stop=False))(),
                                reads=[bst, b_qd4[(qbase // 512)]],
                                writes=[b_psG])
                    return go
                outmm_pending = mk_outmm()
            else:
                outmm_pending = None
            prev_mm2 = None
            for rJ in range(4):
                Jt = 4 * gI + rJ
                co = 128 * rJ
                qoff = qbase + co
                Nq = 512 - co
                ps, bps = nxt()
                if epi_queue:
                    epi_queue.pop(0)()
                p.op(T, (lambda Jt=Jt, qoff=qoff, Nq=Nq, ps=ps: lambda t:
                         t.matmul(ps[:, 0:Nq],
                                  kd[:, 128 * Jt:128 * Jt + 128],
                                  qd[:, qoff:qoff + Nq],
                                  start=True, stop=True))(),
                    reads=[b_kd4[Jt // 4], b_qd4[qoff // 512]],
                    writes=[bps])
                if rJ == 0 and outmm_pending is not None:
                    outmm_pending()
                    outmm_pending = None
                sso = 512 * (rJ % 2)
                bss = b_scsb[rJ % 2]
                p.op(V, (lambda sso=sso, co=co, ps=ps: lambda v:
                         v.tensor_mul(scsb[:, sso + co:sso + co + 128],
                                      ps[:, 0:128], trib[:, :]))(),
                    reads=[bps, b_tabs], writes=[bss])
                if Nq > 128:
                    p.op(SC, (lambda sso=sso, co=co, Nq=Nq, ps=ps:
                              lambda s: s.copy(
                        scsb[:, sso + co + 128:sso + 512],
                        ps[:, 128:Nq]))(),
                        reads=[bps], writes=[bss])
                if prev_mm2 is not None:
                    prev_mm2()
                if epi_queue:
                    epi_queue.pop(0)()
                def mk_mm2(sso=sso, bss=bss, Jt=Jt, rJ=rJ, gI=gI):
                    def go():
                        for ii in range(rJ, 4):
                            It = 4 * gI + ii
                            p.op(T, (lambda sso=sso, ii=ii, Jt=Jt, It=It,
                                     gI=gI, rJ=rJ: lambda t: t.matmul(
                                ps_g[:, 512 * ii:512 * ii + 257],
                                scsb[:, sso + 128 * ii:sso + 128 * ii + 128],
                                vr[:, VRS * Jt:VRS * Jt + 257],
                                start=(gI == 0 and rJ == 0),
                                stop=(Jt == It)))(),
                                reads=[bss, b_vr], writes=[b_psG])
                    return go
                prev_mm2 = mk_mm2()
            prev_mm2()
            for fn in tp_pending:
                fn()
            del tp_pending[:]

            gp = gI % 2
            if gp == 0:
                epb, epw = ep, [b_ep]
                ecb, ecw = ec[:, 0:16], [b_ec]
                zsb, zsw = zst, [b_zst]
            else:
                epb, epw = rtt, [b_rtt[0], b_rtt[1]]
                ecb, ecw = ec[:, 16:32], [b_ec2]
                zsb, zsw = qsb[:, 1024:2048], [b_qsb[1], b_rot[1]]
            psg3 = ps_g.rearrange("p (i s) -> p i s", i=4)
            epg3 = epb.rearrange("p (i s) -> p i s", i=4)
            p.op(SC, (lambda epg3=epg3: lambda s: s.copy(
                epg3, psg3[:, :, 0:256]))(),
                reads=[b_psG], writes=epw)
            p.op(SC, (lambda ecb=ecb: lambda s: s.copy(
                ecb[:, 0:4], psg3[:, :, 256:257]))(),
                reads=[b_psG], writes=ecw)

            def mk_epi(epb=epb, epw=epw, ecb=ecb, ecw=ecw, zsb=zsb, zsw=zsw,
                       gI=gI, h=h, zdst=zdst):
                pieces = []
                pieces.append(lambda: (
                    p.op(V, lambda v: v.scalar_tensor_tensor(
                        ecb[:, 0:4], ecb[:, 0:4], -1.0, ecb[:, 0:4],
                        ALU.mult, ALU.max), reads=[], writes=ecw),
                    p.op(V, lambda v: v.tensor_scalar(
                        ecb[:, 0:4], ecb[:, 0:4], 1.0, None, ALU.max),
                        reads=[], writes=ecw),
                    p.op(V, lambda v: v.reciprocal(ecb[:, 4:8], ecb[:, 0:4]),
                         reads=ecw, writes=ecw)))
                pieces.append(lambda: (
                    p.op(V, lambda v: v.tensor_mul(zsb[:, :], epb[:, :],
                                                   epb[:, :]),
                         reads=epw, writes=zsw),
                    p.op(V, lambda v: v.tensor_reduce(
                        ecb[:, 8:12],
                        zsb[:, :].rearrange("p (i s) -> p i s", i=4),
                        axis=AX.X, op=ALU.add),
                        reads=zsw, writes=ecw)))
                pieces.append(lambda: (
                    p.op(V, lambda v: v.tensor_mul(ecb[:, 12:16], ecb[:, 4:8],
                                                   ecb[:, 4:8]),
                         reads=ecw, writes=ecw),
                    p.op(V, lambda v: v.tensor_mul(ecb[:, 12:16],
                                                   ecb[:, 12:16],
                                                   ecb[:, 8:12]),
                         reads=[], writes=ecw),
                    p.op(V, lambda v: v.tensor_scalar(
                        ecb[:, 12:16], ecb[:, 12:16], 1.0 / 256.0, 1e-5,
                        ALU.mult, ALU.add), reads=[], writes=ecw),
                    p.op(V, lambda v: v.reciprocal(ecb[:, 12:16],
                                                   ecb[:, 12:16]),
                         reads=[], writes=ecw),
                    p.op(SC, lambda s: s.activation(ecb[:, 12:16],
                                                    ecb[:, 12:16], AF.Sqrt),
                         reads=ecw, writes=ecw),
                    p.op(V, lambda v: v.tensor_mul(ecb[:, 12:16],
                                                   ecb[:, 12:16],
                                                   ecb[:, 4:8]),
                         reads=[], writes=ecw)))
                for ii in range(4):
                    It = 4 * gI + ii
                    def zpiece(ii=ii, It=It):
                        p.op(V, (lambda: lambda v: v.scalar_tensor_tensor(
                            zsb[:, 256 * ii:256 * ii + 256],
                            epb[:, 256 * ii:256 * ii + 256],
                            ecb[:, 12 + ii:13 + ii],
                            g1[:, 256 * It:256 * It + 256],
                            ALU.mult, ALU.mult))(),
                            reads=epw + ecw + [b_g1], writes=zsw)
                        for hf in range(2):
                            tslot = 2 * ii + hf
                            def mk_tp(hf=hf, tslot=tslot, ii=ii):
                                def go():
                                    p.op(T, (lambda: lambda t: t.transpose(
                                        ps_t[:, 128 * tslot:128 * tslot + 128],
                                        zsb[:, 256 * ii + 128 * hf:
                                             256 * ii + 128 * hf + 128],
                                        idb[:, :]))(),
                                        reads=zsw + [b_tabs], writes=[b_psT])
                                return go
                            tp_pending.append(mk_tp())
                        if ii == 3:
                            def mk_ev():
                                c0 = (2 * h) % 8
                                dst = zdst.rearrange(
                                    "p (c b t) -> p c b t", c=8, b=16)[
                                    :, c0:c0 + 2, 4 * gI:4 * gI + 4, :]
                                src = ps_t[:, 0:1024].rearrange(
                                    "p (b c t) -> p c b t", b=4, c=2)
                                def go():
                                    p.op(SC, (lambda: lambda s: s.copy(
                                        dst, src))(),
                                        reads=[b_psT],
                                        writes=[b_z[2 * h], b_z[2 * h + 1]])
                                return go
                            tp_pending.append(mk_ev())
                    pieces.append(zpiece)
                return pieces
            epi_queue.extend(mk_epi())
            sprinkle(1)
        if debug and h == 1:
            tap("xn1", xn[:, :], b_xng)
            tap("wqk", wqk[:, :], [b_wqk])
            tap("wvg", wvg[:, :], [b_wvg])
            tap("abt", ab[:, :], [b_ab])
            tap("qd", qd[:, :], b_qd4)
            tap("kd", kd[:, :], b_kd4)
            tap("vr", vr[:, :], [b_vr])
            tap("g1", g1[:, :], [b_g1])
            tap("ep", ep[:, :], [b_ep])
            tap("ec", ec[:, :], [b_ec])
            tap("zst", zst[:, :], [b_zst])
            tap("z1h0", z1[:, :], [b_z[0], b_z[1]])
        if stop_stage == f"h{h}":
            return _finish()

    while pending:
        pending.pop(0)()

    for fn in epi_queue:
        fn()
    del epi_queue[:]
    for fn in tp_pending:
        fn()
    del tp_pending[:]
    tap("qx8", qx8[:, :], [b_qx8])
    tap("qr8", qr8[:, :], [b_qr8])
    tap("z1", z1[:, :], b_z[:8])
    tap("z2", z2[:, :], b_z[8:])

    for c in range(16):
        p.dma(G, wo_b[:, 1024 * c:1024 * c + 1024],
              d_wo[128 * c:128 * c + 128, :],
              writes=(b_xng + [b_wo] if c == 0 else [b_wo]))
    def emit_rs_bcast():
        for c in range(NC):
            p.op(T, (lambda c=c: lambda t: t.matmul(
                ps_x[0:1, 0:128], rscol[:, c:c + 1], idf[:, :],
                start=True, stop=True))(),
                reads=[b_rsc, b_tabs], writes=[b_psX])
            p.op(SC, lambda s2: s2.copy(row[:, 0:128], ps_x[0:1, 0:128]),
                 reads=[b_psX], writes=[b_row])
            p.op(T, lambda t: t.matmul(ps_x[:, 0:128], onr[:, :],
                                       row[:, 0:128], start=True, stop=True),
                 reads=[b_row, b_tabs], writes=[b_psX])
            p.op(SC, (lambda c=c: lambda s2: s2.copy(
                rsb2[:, 128 * c:128 * c + 128], ps_x[:, 0:128]))(),
                reads=[b_psX], writes=[b_bc])
    emit_rs_bcast()
    for c in range(NC):
        p.op(V, (lambda c=c: lambda v: v.tensor_scalar(
            qrb[:, 1024 * c:1024 * c + 1024],
            qr8[:, 1024 * c:1024 * c + 1024], -128.0, None, ALU.add))(),
            reads=[b_qr8],
            writes=([b_vr, b_g1, b_qrb] if c == 0 else [b_qrb]))

    b_fin = [Buf("fin0"), Buf("fin1")]
    b_fot = [Buf("fot0"), Buf("fot1")]
    for i in range(NT):
        sl3 = i % 2
        go = 1024 * sl3
        if sl3 == 0:
            xtile = xrl[:, 0:1024]
            bx = [b_xs[0], b_xs[1]]
        else:
            xtile = ufuk[:, 0:1024]
            bx = [b_uf[0], b_uk[0]]
        p.dma(SY, xtile, d_xtm[128 * i:128 * i + 128, :], writes=bx)
        def em_womm(i=i, go=go):
            def f(t):
                ins = None
                for c in range(16):
                    ztens = z1 if c < 8 else z2
                    zc = (c % 8) * S + 128 * i
                    for hf in range(2):
                        ins = t.matmul(
                            ps_g[:, go + 512 * hf:go + 512 * hf + 512],
                            ztens[:, zc:zc + 128],
                            wo_b[:, 1024 * c + 512 * hf:1024 * c + 512 * hf + 512],
                            start=(c == 0), stop=(c == 15))
                return ins
            return f
        bqx = b_qxb[sl3]
        qxv = qxb[:, 1024 * sl3:1024 * sl3 + 1024].rearrange(
            "p (c f) -> p c f", c=8)
        qx8v = qx8.rearrange("p (c t) -> p c t", c=8)[:, :, 128 * i:128 * i + 128]
        p.op(SC, (lambda qxv=qxv, qx8v=qx8v: lambda s2: s2.activation(
            qxv, qx8v, AF.Identity, bias=cc[:, 10:11]))(),
            reads=[b_qx8, b_cc],
            writes=([b_qsb[0], b_qsb[1], b_rot[0], b_rot[1], bqx]
                    if i < 2 else [bqx]))
        for hf, (psf, bpsf) in enumerate(((ps_a, b_psA), (ps_b, b_psB))):
            def em_fmm(i=i, sl3=sl3, hf=hf, psf=psf):
                def f(t):
                    ins = None
                    for c in range(NC):
                        ins = t.matmul(
                            psf[:, 0:512],
                            qxb[:, 1024 * sl3 + 128 * c:1024 * sl3 + 128 * c + 128],
                            qrb[:, 1024 * c + 512 * hf:1024 * c + 512 * hf + 512],
                            start=(c == 0), stop=(c == NC - 1))
                    return ins
                return f
            p.op(T, em_fmm(), reads=[bqx, b_qrb], writes=[bpsf])
        bg2 = b_g2[sl3]
        for hf, (psf, bpsf) in enumerate(((ps_a, b_psA), (ps_b, b_psB))):
            p.op(V, (lambda hf=hf, psf=psf: lambda v: v.tensor_mul(
                fin[:, 512 * hf:512 * hf + 512], psf[:, 0:512],
                rsb2[:, 512 * hf:512 * hf + 512]))(),
                reads=[bpsf, b_bc],
                writes=([b_ab, b_fin[hf]] if i == 0 else [b_fin[hf]]))
            p.op(SC, (lambda hf=hf, sl3=sl3, i=i: lambda s: s.activation(
                g2t[:, 1024 * sl3 + 512 * hf:1024 * sl3 + 512 * hf + 512],
                fin[:, 512 * hf:512 * hf + 512], AF.Silu,
                scale=st2[:, 32 + i:33 + i]))(),
                reads=[b_fin[hf], b_st2],
                writes=(b_qd4 + b_kd4 + [bg2] if i < 2 else [bg2]))
        p.op(T, em_womm(), reads=[b_wo] + b_z, writes=[b_psG])
        for hf in range(2):
            p.op(V, (lambda hf=hf, go=go, sl3=sl3: lambda v: v.tensor_mul(
                fin[:, 512 * hf:512 * hf + 512],
                ps_g[:, go + 512 * hf:go + 512 * hf + 512],
                g2t[:, 1024 * sl3 + 512 * hf:1024 * sl3 + 512 * hf + 512]))(),
                reads=[b_psG, bg2], writes=[b_fin[hf]])
            p.op(G, (lambda hf=hf, sl3=sl3, xtile=xtile: lambda g: g.tensor_add(
                fot[:, 512 * hf:512 * hf + 512],
                fin[:, 512 * hf:512 * hf + 512],
                xtile[:, 512 * hf:512 * hf + 512]))(),
                reads=[b_fin[hf]] + bx,
                writes=([b_ab, b_fot[hf]] if i == 0 else [b_fot[hf]]))
            p.dma(SY, d_out[128 * i:128 * i + 128, 512 * hf:512 * hf + 512],
                  fot[:, 512 * hf:512 * hf + 512], reads=[b_fot[hf]])

    p.raw(G, lambda g: g.nop(), waits=p.dma_total_events())

    with nc.Block() as block:
        p.build(block)
    return nc


_NC_CACHE = None


def kernel(L0_attn_x, L0_attn_wq, L0_attn_wk, L0_attn_wv, L0_attn_wg,
           L0_attn_wo, L0_ffn_rhs):
    global _NC_CACHE
    tabs = _host_tables()
    if _NC_CACHE is None:
        _NC_CACHE = build_nc()
    nc = _NC_CACHE
    x = np.ascontiguousarray(np.asarray(L0_attn_x, dtype=np.float32))
    wq = np.ascontiguousarray(np.asarray(L0_attn_wq, dtype=np.float32))
    wk = np.ascontiguousarray(np.asarray(L0_attn_wk, dtype=np.float32))
    wv = np.ascontiguousarray(np.asarray(L0_attn_wv, dtype=np.float32))
    wg = np.ascontiguousarray(np.asarray(L0_attn_wg, dtype=np.float32))
    wo = np.ascontiguousarray(np.asarray(L0_attn_wo, dtype=np.float32))
    rhs = np.ascontiguousarray(np.asarray(L0_ffn_rhs, dtype=np.float32))
    rhsT = np.ascontiguousarray(rhs.T)
    in_maps = []
    for c in range(NCORES):
        xc = np.ascontiguousarray(x[c])
        in_maps.append(dict(
            x_tm=xc, x_fm=np.ascontiguousarray(xc.T),
            wq=wq, wk=wk, wv=wv, wg=wg, wo=wo, rhs=rhs, rhsT=rhsT,
            cosfm=tabs["cosfm"], sinfm=tabs["sinfm"],
            aba=tabs["aba"], abb=tabs["abb"], Pt=tabs["Pt"],
            tri=tabs["tri"], ident=tabs["ident"]))
    res = run_bass_kernel_spmd(nc, in_maps, core_ids=list(range(NCORES)))
    return np.stack([res.results[c]["out"] for c in range(NCORES)])


# version 36
# speedup vs baseline: 1.0171x; 1.0160x over previous
import numpy as np
import ml_dtypes
import concourse.bass as bass
import concourse.mybir as mybir
from concourse.bass_utils import run_bass_kernel_spmd

F32 = mybir.dt.float32
BF16 = mybir.dt.bfloat16
F16 = mybir.dt.float16
U8 = mybir.dt.uint8
AF = mybir.ActivationFunctionType
ALU = mybir.AluOpType
AX = mybir.AxisListType
BFNP = ml_dtypes.bfloat16

NCORES = 8
S = 2048
D = 1024
H = 8
NT = 16
NC = 8
SBIG = 1.0e8
FOLDS = [128, 64, 32, 16, 8, 4, 2, 1]
VRS = 272

ENGINES = ("tensor", "vector", "scalar", "gpsimd", "sync")
V, SC, G, T, SY = "vector", "scalar", "gpsimd", "tensor", "sync"


class Event:
    __slots__ = ("sem", "value")
    def __init__(self, sem, value):
        self.sem = sem
        self.value = value


class _Sem:
    __slots__ = ("handle", "count", "name")
    def __init__(self, handle, name):
        self.handle = handle
        self.count = 0
        self.name = name


class Buf:
    __slots__ = ("w", "rs", "name")
    def __init__(self, name=""):
        self.w = None
        self.rs = []
        self.name = name


NDMALANES = 8


class Prog:
    def __init__(self, nc):
        self.nc = nc
        self.sems = {e: _Sem(nc.alloc_semaphore(f"p_{e}"), e) for e in ENGINES}
        self.dlanes = {e: [_Sem(nc.alloc_semaphore(f"d_{e}{i}"), f"d_{e}{i}")
                          for i in range(NDMALANES)]
                       for e in (SY, G, SC)}
        self.drr = {e: 0 for e in (SY, G, SC)}
        self.ops = {e: [] for e in ENGINES}
        self.n_ops = 0

    def _gather(self, reads, writes, waits):
        ws = list(waits)
        for b in reads:
            if b.w is not None:
                ws.append(b.w)
        for b in writes:
            if b.w is not None:
                ws.append(b.w)
            ws.extend(b.rs)
        return ws

    def _add(self, eng, emit, ws, inc):
        s = self.sems[eng]
        ev = None
        if inc:
            s.count += inc
            ev = Event(s, s.count)
        self.ops[eng].append((tuple(w for w in ws if w is not None), emit, ev, inc))
        self.n_ops += 1
        return ev

    def _commit(self, ev, reads, writes):
        for b in reads:
            b.rs.append(ev)
        for b in writes:
            b.w = ev
            b.rs = []

    def op(self, eng, emit, reads=(), writes=(), waits=()):
        ev = self._add(eng, emit, self._gather(reads, writes, waits), 1)
        self._commit(ev, reads, writes)
        return ev

    def dma(self, eng, out, in_, reads=(), writes=(), waits=(), tr=False):
        lane = self.dlanes[eng][self.drr[eng]]
        self.drr[eng] = (self.drr[eng] + 1) % NDMALANES
        ws = self._gather(reads, writes, waits)
        if lane.count:
            ws.append(Event(lane, lane.count))
        if tr:
            def emit(e, out=out, in_=in_):
                return e.dma_start_transpose(out, in_)
        else:
            def emit(e, out=out, in_=in_):
                return e.dma_start(out=out, in_=in_)
        lane.count += 16
        ev = Event(lane, lane.count)
        self.ops[eng].append((tuple(w for w in ws if w is not None),
                              emit, ev, 16))
        self.n_ops += 1
        self._commit(ev, reads, writes)
        return ev

    def dma_total_events(self):
        evs = []
        for e in (SY, G, SC):
            for lane in self.dlanes[e]:
                if lane.count:
                    evs.append(Event(lane, lane.count))
        return evs

    def raw(self, eng, emit, waits=()):
        return self._add(eng, emit, list(waits), 0)

    def build(self, block):
        for eng_name in ENGINES:
            if not self.ops[eng_name]:
                continue
            def run(eng, eng_name=eng_name):
                seen = {}
                my = self.sems[eng_name]
                for waits, emit, ev, inc in self.ops[eng_name]:
                    need = {}
                    for w in waits:
                        if w.sem is my and eng_name in ("sync", "tensor"):
                            continue
                        k = id(w.sem)
                        if w.value > need.get(k, (None, -1))[1]:
                            need[k] = (w.sem, w.value)
                    for sem, val in need.values():
                        if seen.get(id(sem), -1) >= val:
                            continue
                        eng.wait_ge(sem.handle, val)
                        seen[id(sem)] = val
                    ins = emit(eng)
                    if ev is not None:
                        ins.then_inc(ev.sem.handle, inc)
            getattr(block, eng_name)(run)


_TABLES = None

def _host_tables():
    global _TABLES
    if _TABLES is not None:
        return _TABLES
    m_idx = np.arange(64, dtype=np.float64)
    theta = 10000.0 ** (-m_idx / 63.0)
    t_idx = np.arange(S, dtype=np.float64)
    ang = t_idx[:, None] * theta[None, :]
    cos_fm = np.ascontiguousarray(np.repeat(np.cos(ang), 2, axis=1).T.astype(np.float32))
    sin_fm = np.ascontiguousarray(np.repeat(np.sin(ang), 2, axis=1).T.astype(np.float32))
    d_h = 1.0 - 2.0 ** (-5.0 - np.arange(H, dtype=np.float64))
    dpow = d_h[:, None] ** t_idx[None, :]
    msum = (1.0 - d_h[:, None] ** (t_idx[None, :] + 1.0)) / (1.0 - d_h[:, None])
    a_tab = (dpow / np.sqrt(msum)).astype(np.float32)
    b_tab = ((1.0 / dpow) * (128.0 ** -0.5)).astype(np.float32)
    ab_a = np.ascontiguousarray(np.broadcast_to(a_tab[:, None, :], (H, 128, S)))
    ab_b = np.ascontiguousarray(np.broadcast_to(b_tab[:, None, :], (H, 128, S)))
    P = np.zeros((128, 128), dtype=np.float32)
    for m2 in range(64):
        P[2 * m2, 2 * m2 + 1] = -1.0
        P[2 * m2 + 1, 2 * m2] = 1.0
    _TABLES = dict(
        cosfm=cos_fm, sinfm=sin_fm,
        aba=ab_a.astype(np.float32), abb=ab_b.astype(np.float32),
        Pt=np.ascontiguousarray(P.T),
        tri=np.ascontiguousarray(
            (np.arange(128)[:, None] <= np.arange(128)[None, :]).astype(np.float32)),
        ident=np.eye(128, dtype=np.float32))
    return _TABLES


def build_nc(stop_stage=None, debug=False):
    nc = bass.Bass("TRN2", target_bir_lowering=False)

    d_xtm = nc.dram_tensor("x_tm", [S, D], F32, kind="ExternalInput").ap()
    d_xfm = nc.dram_tensor("x_fm", [D, S], F32, kind="ExternalInput").ap()
    d_wq = nc.dram_tensor("wq", [D, D], F32, kind="ExternalInput").ap()
    d_wk = nc.dram_tensor("wk", [D, D], F32, kind="ExternalInput").ap()
    d_wv = nc.dram_tensor("wv", [D, 2 * D], F32, kind="ExternalInput").ap()
    d_wg = nc.dram_tensor("wg", [D, 2 * D], F32, kind="ExternalInput").ap()
    d_wo = nc.dram_tensor("wo", [2 * D, D], F32, kind="ExternalInput").ap()
    d_rhsT = nc.dram_tensor("rhsT", [D, D], F32, kind="ExternalInput").ap()
    d_rhs = nc.dram_tensor("rhs", [D, D], F32, kind="ExternalInput").ap()
    d_cos = nc.dram_tensor("cosfm", [128, S], F32, kind="ExternalInput").ap()
    d_sin = nc.dram_tensor("sinfm", [128, S], F32, kind="ExternalInput").ap()
    d_aba = nc.dram_tensor("aba", [H, 128, S], F32, kind="ExternalInput").ap()
    d_abb = nc.dram_tensor("abb", [H, 128, S], F32, kind="ExternalInput").ap()
    d_Pt = nc.dram_tensor("Pt", [128, 128], F32, kind="ExternalInput").ap()
    d_tri = nc.dram_tensor("tri", [128, 128], F32, kind="ExternalInput").ap()
    d_id = nc.dram_tensor("ident", [128, 128], F32, kind="ExternalInput").ap()
    d_out = nc.dram_tensor("out", [S, D], F32, kind="ExternalOutput").ap()

    _off = [0]
    def region(nbytes, align=64):
        o = (_off[0] + align - 1) // align * align
        _off[0] = o + nbytes
        return o
    _arena = [None]
    def view(off, cols, dtype, rows=128):
        esz = {F32: 4, BF16: 2, F16: 2, U8: 1}[dtype]
        return _arena[0][0:rows, off:off + cols * esz].bitcast(dtype)

    o_xn = region(32768)
    o_qx = region(16384)
    o_z1 = region(32768)
    o_z2 = region(32768)
    o_cs = region(8192)
    o_bc = region(4096)
    o_qr = region(8192)
    o_ab = region(8192)
    o_qd = region(4096)
    o_kd = region(4096)
    o_qs = region(4096)
    o_vr = region(2 * 16 * VRS)
    o_g1 = region(8192)
    o_sc = region(2048)
    o_wqk = region(4096)
    o_wvg = region(8192)
    o_xs = region(4096)
    o_uf = region(2048)
    o_uk = region(2048)
    o_ktm = region(4096)
    o_ts = region(2048)
    o_rt = region(2048)
    o_row = region(2048)
    o_stc = region(512)
    o_st2 = region(512)
    o_st3 = region(256)
    o_rsc = region(64)
    o_cc = region(64)
    o_pt = region(256)
    o_tri = region(256)
    o_idb = region(256)
    o_idf = region(512)
    o_or = region(512)
    o_ep = region(2048)
    o_ec = region(128)
    o_zs = region(2048)
    ARENA = (_off[0] + 63) // 64 * 64
    avail = nc.sbuf_bytes_remaining
    assert ARENA <= avail, (ARENA, avail)
    _arena[0] = nc.alloc_sbuf_tensor("arena", [128, ARENA], U8).ap()

    xn = view(o_xn, NC * S, BF16)
    wo_b = view(o_xn, 16 * D, BF16)
    qx8 = view(o_qx, NC * S, U8)
    z1 = view(o_z1, 8 * S, BF16)
    z2 = view(o_z2, 8 * S, BF16)
    cs = view(o_cs, 2 * S, BF16)
    bc = view(o_bc, 1024, F32)
    qr8 = view(o_qr, 8192, U8)
    ab = view(o_ab, 2 * S, BF16)
    qd = view(o_qd, S, BF16)
    kd = view(o_kd, S, BF16)
    qsb = view(o_qs, S, BF16)
    vr = view(o_vr, 2 * 16 * VRS // 2, BF16)
    g1 = view(o_g1, 16 * 256, BF16)
    scsb = view(o_sc, 1024, BF16)
    wqk = view(o_wqk, 2048, BF16)
    wvg = view(o_wvg, 4096, BF16)
    xs = view(o_xs, 1024, F32)
    uf = view(o_uf, 512, F32)
    uk = view(o_uk, 512, F32)
    ufuk = view(o_uf, 1024, F32)
    b0h = view(o_uf, 1024, F16)
    bwh = view(o_uk, 1024, F16)
    thx = [view(o_xs, 1024, F16), view(o_bc, 1024, F16)]
    xqx = [view(o_xs, 1024, F32), view(o_bc, 1024, F32)]
    rowq = view(o_row, 1024, BF16)
    qsf = view(o_qs, 1024, F32)
    ktm = view(o_ktm, 16 * 128, BF16)
    tst = view(o_ts, 1024, BF16)
    rtt = view(o_rt, 1024, BF16)
    row = view(o_row, 512, F32, rows=1)
    stc = view(o_stc, 64, F32)
    st2 = view(o_st2, 64, F32)
    st3 = view(o_st3, 32, F32)
    rscol = view(o_rsc, 16, F32)
    cc = view(o_cc, 16, F32)
    ptb = view(o_pt, 128, BF16)
    trib = view(o_tri, 128, BF16)
    idb = view(o_idb, 128, BF16)
    idf = view(o_idf, 128, F32)
    onr = view(o_or, 128, F32, rows=1)
    ep = view(o_ep, 1024, BF16)
    ec = view(o_ec, 32, F32)
    zst = view(o_zs, 1024, BF16)
    qrb = view(o_vr, 8192, BF16)
    qxb = view(o_qs, S, BF16)
    g2t = view(o_qd, 2048, BF16)
    rsb2 = view(o_bc, 1024, F32)
    xrl = view(o_xs, 1024, F32)

    fin = view(o_ab, 1024, F32)
    fot = view(o_ab + 4096, 1024, F32)

    ps_a = nc.alloc_psum_tensor("psA", [128, 512], F32).ap()
    ps_b = nc.alloc_psum_tensor("psB", [128, 512], F32).ap()
    ps_g = nc.alloc_psum_tensor("psG", [128, 2048], F32).ap()
    ps_x = nc.alloc_psum_tensor("psX", [128, 512], F32).ap()
    ps_t = nc.alloc_psum_tensor("psT", [128, 1024], BF16).ap()

    p = Prog(nc)

    _taps = {}
    def tap(name, ap, reads):
        if not debug or (isinstance(debug, (set, frozenset, list, tuple))
                         and name not in debug):
            return
        shp = list(ap.shape)
        dt_ = ap.dtype
        o_ = nc.dram_tensor("tap_" + name, shp, dt_, kind="ExternalOutput").ap()
        p.dma(SY, o_, ap, reads=list(reads))

    b_xng = [Buf("xn0"), Buf("xn1"), Buf("xn2"), Buf("xn3")]
    b_qx8 = Buf("qx8")
    b_z = [Buf(f"z{i}") for i in range(16)]
    b_cs = Buf("cs"); b_bc = Buf("bc"); b_qr8 = Buf("qr8"); b_ab = Buf("ab")
    b_qd4 = [Buf(f"qd{t}") for t in range(4)]
    b_kd4 = [Buf(f"kd{t}") for t in range(4)]
    b_ktm = Buf("ktm")
    b_qsb = [Buf("qsb0"), Buf("qsb1")]
    b_rot = [Buf("rot0"), Buf("rot1")]
    b_vr = Buf("vr"); b_g1 = Buf("g1")
    b_scsb = [Buf("sc0"), Buf("sc1")]
    b_wqk = Buf("wqk"); b_wvg = Buf("wvg")
    b_xs = [Buf("xs0"), Buf("xs1")]
    b_uf = [Buf("uf0"), Buf("uf1")]
    b_uk = [Buf("uk0"), Buf("uk1")]
    b_tst = [Buf("ts0"), Buf("ts1")]
    b_rtt = [Buf("rt0"), Buf("rt1")]
    b_row = Buf("row")
    b_stc = Buf("stc"); b_st2 = Buf("st2"); b_st3 = Buf("st3")
    b_rsc = Buf("rscol")
    b_cc = Buf("cc"); b_tabs = Buf("tabs")
    b_ep = Buf("ep"); b_ec = Buf("ec"); b_ec2 = Buf("ec2"); b_zst = Buf("zst")
    b_psA = Buf("psA"); b_psB = Buf("psB"); b_psG = Buf("psG")
    b_psX = Buf("psX"); b_psT = Buf("psT")
    b_wo = Buf("wo_b"); b_qrb = Buf("qrb")
    b_qxb = [Buf("qxb0"), Buf("qxb1")]
    b_g2 = [Buf("g2a"), Buf("g2b")]

    def em_consts(v):
        v.memset(cc[:, 0:1], SBIG)
        v.memset(cc[:, 9:10], 128.5)
        v.memset(cc[:, 10:11], -128.0)
        ins = None
        for i, b in enumerate(FOLDS):
            ins = v.memset(cc[:, 1 + i:2 + i], -SBIG * float(b))
        return ins
    p.op(V, em_consts, writes=[b_cc])
    p.op(V, lambda v: v.memset(onr[:, :], 1.0), writes=[b_tabs])
    def em_vrones(v):
        ins = None
        for i in range(16):
            ins = v.memset(vr[:, VRS * i + 256:VRS * i + 257], 1.0)
            ins = v.memset(vr[:, VRS * i + 257:VRS * i + 260], 0.0)
        return ins
    p.op(V, em_vrones, writes=[b_vr])
    p.dma(G, cs[:, 0:S], d_cos, writes=[b_cs])
    p.dma(G, cs[:, S:2 * S], d_sin, writes=[b_cs])
    p.dma(G, ptb[:, :], d_Pt, writes=[b_tabs])
    p.dma(G, trib[:, :], d_tri, writes=[b_tabs])
    p.dma(G, idb[:, :], d_id, writes=[b_tabs])
    p.dma(SY, idf[:, :], d_id, writes=[b_tabs])

    _xpb = [b_xs[0], b_xs[1]]
    _bcb = [b_bc]
    def _xpw(slot):
        return _xpb if slot == 0 else _bcb
    def _stage(slot):
        return (tst, b_tst[0]) if slot == 0 else (rowq, b_row)

    _p0bufs = [
        (xs[:, 0:1024], [b_xs[0], b_xs[1]]),
        (xqx[1], [b_bc]),
        (ufuk[:, 0:1024], [b_uf[0], b_uk[0]]),
        (qsf, [b_qsb[0], b_qsb[1], b_rot[0], b_rot[1]]),
    ]

    def _statmath(tg):
        a = 4 * tg
        m_ = st2[:, a:a + 4]
        rs_ = st2[:, 16 + a:16 + a + 4]
        ls_ = st2[:, 32 + a:32 + a + 4]
        sc_ = st2[:, 48 + a:48 + a + 4]
        e_ = ec[:, a:a + 4]
        p.op(V, lambda v: v.tensor_scalar(m_, stc[:, a:a + 4], 1.0 / D, None,
                                          ALU.mult),
             reads=[b_stc], writes=[b_st2])
        p.op(V, lambda v: v.tensor_mul(sc_, m_, m_),
             reads=[b_st2], writes=[b_st2])
        p.op(V, lambda v: v.scalar_tensor_tensor(
            sc_, stc[:, 16 + a:16 + a + 4], 1.0 / D, sc_,
            ALU.mult, ALU.subtract), reads=[b_stc, b_st2], writes=[b_st2])
        p.op(V, lambda v: v.tensor_scalar(sc_, sc_, 1e-5, None, ALU.add),
             reads=[b_st2], writes=[b_st2])
        p.op(V, lambda v: v.reciprocal(rs_, sc_),
             reads=[b_st2], writes=[b_st2])
        p.op(SC, lambda s: s.activation(rs_, rs_, AF.Sqrt),
             reads=[b_st2], writes=[b_st2])
        p.op(V, lambda v: v.tensor_sub(sc_, stc[:, 32 + a:32 + a + 4], m_),
             reads=[b_stc, b_st2], writes=[b_st2])
        p.op(V, lambda v: v.tensor_sub(e_, m_, stc[:, 48 + a:48 + a + 4]),
             reads=[b_stc, b_st2], writes=[b_ec])
        p.op(V, lambda v: v.tensor_max(sc_, sc_, e_),
             reads=[b_ec], writes=[b_st2])
        p.op(V, lambda v: v.tensor_mul(sc_, sc_, rs_),
             reads=[b_st2], writes=[b_st2])
        p.op(V, lambda v: v.tensor_scalar(sc_, sc_, 1e-12, None, ALU.max),
             reads=[b_st2], writes=[b_st2])
        p.op(V, lambda v: v.tensor_scalar(ls_, sc_, 1.0 / 127.0, None,
                                          ALU.mult),
             reads=[b_st2], writes=[b_st2])
        p.op(V, lambda v: v.reciprocal(e_, sc_),
             reads=[b_st2], writes=[b_ec])
        p.op(V, lambda v: v.scalar_tensor_tensor(
            st3[:, a:a + 4], e_, 127.0, rs_, ALU.mult, ALU.mult),
            reads=[b_ec, b_st2], writes=[b_st3])
        p.op(V, lambda v: v.tensor_mul(st3[:, 16 + a:16 + a + 4], m_,
                                       st3[:, a:a + 4]),
             reads=[b_st2, b_st3], writes=[b_st3])
        p.op(V, lambda v: v.tensor_scalar(
            st3[:, 16 + a:16 + a + 4], st3[:, 16 + a:16 + a + 4],
            -128.5, None, ALU.add), reads=[b_st3], writes=[b_st3])
        p.op(V, lambda v: v.tensor_mul(sc_, m_, rs_),
             reads=[b_st2], writes=[b_st2])

    for tg in range(4):
        for k in range(4):
            i = 4 * tg + k
            sl, bsl = _p0bufs[k]
            p.dma(SY, sl, d_xtm[128 * i:128 * i + 128, :], writes=bsl)
            p.op(V, (lambda sl=sl, i=i: lambda v: v.reduce_max(
                stc[:, 32 + i:33 + i], sl, axis=AX.X))(),
                reads=bsl, writes=[b_stc])
            p.op(V, (lambda sl=sl, i=i: lambda v: v.tensor_reduce(
                stc[:, 48 + i:49 + i], sl, axis=AX.X, op=ALU.min))(),
                reads=bsl, writes=[b_stc])
            p.op(V, (lambda sl=sl, i=i: lambda v: v.reduce_sum(
                stc[:, i:i + 1], sl, axis=AX.X))(),
                reads=bsl, writes=[b_stc])
            p.op(SC, (lambda sl=sl: lambda s: s.activation(
                ep[:, 0:1024], sl, AF.Square))(),
                reads=bsl, writes=[b_ep])
            p.op(V, (lambda i=i: lambda v: v.reduce_sum(
                stc[:, 16 + i:17 + i], ep[:, 0:1024], axis=AX.X))(),
                reads=[b_ep], writes=[b_stc])
        _statmath(tg)
        for k in range(4):
            i = 4 * tg + k
            sl, bsl = _p0bufs[k]
            stg, bstg = _stage(k % 2)
            p.op(V, (lambda i=i, sl=sl, stg=stg: lambda v: v.tensor_scalar(
                stg[:, 0:1024], sl,
                st2[:, 16 + i:17 + i], st2[:, 48 + i:49 + i],
                ALU.mult, ALU.subtract))(),
                reads=bsl + [b_st2], writes=[bstg])
            for j in range(8):
                p.op(T, (lambda j=j, stg=stg: lambda t: t.transpose(
                    ps_t[:, 128 * j:128 * j + 128],
                    stg[:, 128 * j:128 * j + 128], idb[:, :]))(),
                    reads=[bstg, b_tabs], writes=[b_psT])
            dst = xn.rearrange("p (c t) -> p c t", c=NC)[
                :, :, 128 * i:128 * i + 128]
            src = ps_t[:, 0:1024].rearrange("p (j c) -> p j c", j=8)
            p.op(SC, (lambda dst=dst, src=src: lambda s: s.copy(dst, src))(),
                 reads=[b_psT], writes=[b_xng[tg]])

    def emit_col2bcast(colap, breads, dst_off, tg):
        for j in range(4):
            i = 4 * tg + j
            p.op(T, (lambda i=i, j=j, colap=colap: lambda t: t.matmul(
                ps_x[0:1, 128 * j:128 * j + 128], colap[:, i:i + 1], idf[:, :],
                start=True, stop=True))(),
                reads=list(breads) + [b_tabs], writes=[b_psX])
        p.op(SC, lambda s2: s2.copy(row[:, 0:512], ps_x[0:1, 0:512]),
             reads=[b_psX], writes=[b_row])
        p.op(T, lambda t: t.matmul(ps_x[:, 0:512], onr[:, :], row[:, :],
                                   start=True, stop=True),
             reads=[b_row, b_tabs], writes=[b_psX])
        p.op(SC, (lambda dst_off=dst_off: lambda s2: s2.copy(
            bc[:, dst_off:dst_off + 512], ps_x[:, 0:512]))(),
            reads=[b_psX], writes=[b_bc])

    def _finish():
        while pending:
            pending.pop(0)()
        for fn in epi_queue:
            fn()
        del epi_queue[:]
        for fn in tp_pending:
            fn()
        del tp_pending[:]
        p.raw(G, lambda g: g.nop(), waits=p.dma_total_events())
        all_sems = [s.handle for s in p.sems.values()]
        for lanes in p.dlanes.values():
            all_sems.extend(l.handle for l in lanes)
        for sh in all_sems:
            nc.gpsimd.sem_clear(sh)
        nc.all_engine_barrier()
        with nc.Block() as block:
            p.build(block)
        return nc

    tap("xn", xn[:, :], b_xng)
    tap("bc", bc[:, :], [b_bc])

    pending = []

    _spg = [True]
    def sprinkle(n):
        if not _spg[0]:
            return
        k = 0
        while pending and k < n:
            pending.pop(0)()
            k += 1

    def emit_fold_chain(ufl, tsl, buf_, bts_, eng=V):
        for bi, bv in enumerate(FOLDS):
            p.op(eng, (lambda bv=bv, ufl=ufl, tsl=tsl: lambda e:
                       e.tensor_scalar(
                tsl, ufl, float(bv), -float(bv), ALU.is_ge, ALU.mult))(),
                reads=[buf_], writes=[bts_])
            p.op(eng, (lambda ufl=ufl, tsl=tsl: lambda e: e.tensor_add(
                ufl, ufl, tsl))(),
                reads=[bts_], writes=[buf_])

    pidx = [0]

    def _fold_chain(slot, buf_, buk_):
        th = thx[slot]
        bx = _xpw(slot)
        first = True
        for bv in FOLDS:
            bsrc = b0h if first else bwh
            p.op(V, (lambda bv=bv, bsrc=bsrc, th=th: lambda v:
                     v.tensor_scalar(
                th[:, 0:1024], bsrc[:, 0:1024], float(bv), -float(bv),
                ALU.is_ge, ALU.mult))(),
                reads=[buf_ if first else buk_], writes=bx)
            p.op(V, (lambda bsrc=bsrc, th=th: lambda v: v.tensor_add(
                bwh[:, 0:1024], bsrc[:, 0:1024], th[:, 0:1024]))(),
                reads=bx + ([buf_] if first else []), writes=[buk_])
            first = False

    def emit_xq_v(i):
        slot = i % 2
        xsl = xqx[slot]
        bx = _xpw(slot)
        buf_, buk_ = b_uf[0], b_uk[0]
        stg, bstg = _stage(slot)
        p.dma(SY, xsl, d_xtm[128 * i:128 * i + 128, :], writes=bx)
        p.op(V, (lambda i=i, xsl=xsl: lambda v: v.tensor_scalar(
            b0h[:, 0:1024], xsl, st3[:, i:i + 1], st3[:, 16 + i:17 + i],
            ALU.mult, ALU.subtract))(),
            reads=bx + [b_st3], writes=[buf_])
        _fold_chain(slot, buf_, buk_)
        p.op(V, (lambda stg=stg: lambda v: v.tensor_sub(
            stg[:, 0:1024], b0h[:, 0:1024], bwh[:, 0:1024]))(),
            reads=[buf_, buk_], writes=[bstg])

    def emit_xq_t(i):
        slot = i % 2
        stg, bstg = _stage(slot)
        for j in range(8):
            p.op(T, (lambda j=j, stg=stg: lambda t: t.transpose(
                ps_t[:, 128 * j:128 * j + 128],
                stg[:, 128 * j:128 * j + 128], idb[:, :]))(),
                reads=[bstg, b_tabs], writes=[b_psT])
        dst = qx8.rearrange("p (c t) -> p c t", c=NC)[
            :, :, 128 * i:128 * i + 128]
        src = ps_t[:, 0:1024].rearrange("p (j c) -> p j c", j=8)
        p.op(SC, (lambda dst=dst, src=src: lambda s: s.copy(dst, src))(),
             reads=[b_psT], writes=[b_qx8])

    def emit_rhsA(c):
        bxs = [b_xs[0], b_xs[1]]
        xsl = xs[:, 0:1024]
        p.dma(SY, xsl, d_rhsT[128 * c:128 * c + 128, :], writes=bxs)
        p.op(V, (lambda c=c, xsl=xsl: lambda v: v.reduce_max(
            ec[:, 24:25], xsl, axis=AX.X, apply_absolute_value=True))(),
            reads=bxs, writes=[b_ec2])
        p.op(V, (lambda c=c: lambda v: v.tensor_scalar(
            rscol[:, c:c + 1], ec[:, 24:25], 1e-12, 1.0 / 127.0,
            ALU.max, ALU.mult))(),
            reads=[b_ec2], writes=[b_rsc])

    def emit_rhsB_v(cf):
        slot = cf % 2
        xsl = xqx[slot]
        bx = _xpw(slot)
        buf_, buk_ = b_uf[0], b_uk[0]
        stg, bstg = _stage(slot)
        p.dma(SY, xsl, d_rhsT[128 * cf:128 * cf + 128, :], writes=bx)
        p.op(V, (lambda cf=cf, xsl=xsl: lambda v: v.tensor_scalar(
            b0h[:, 0:1024], xsl,
            rscol[:, 8 + cf:9 + cf], cc[:, 9:10],
            ALU.mult, ALU.add))(),
            reads=bx + [b_rsc, b_cc], writes=[buf_])
        _fold_chain(slot, buf_, buk_)
        p.op(V, (lambda stg=stg: lambda v: v.tensor_sub(
            stg[:, 0:1024], b0h[:, 0:1024], bwh[:, 0:1024]))(),
            reads=[buf_, buk_], writes=[bstg])

    def emit_rhsB_t(cf):
        slot = cf % 2
        stg, bstg = _stage(slot)
        for j in range(8):
            p.op(T, (lambda j=j, stg=stg: lambda t: t.transpose(
                ps_t[:, 128 * j:128 * j + 128],
                stg[:, 128 * j:128 * j + 128], idb[:, :]))(),
                reads=[bstg, b_tabs], writes=[b_psT])
        dst = qr8.rearrange("p (c f) -> p c f", c=NC)[
            :, :, 128 * cf:128 * cf + 128]
        src = ps_t[:, 0:1024].rearrange("p (j c) -> p j c", j=8)
        p.op(SC, (lambda dst=dst, src=src: lambda s: s.copy(dst, src))(),
             reads=[b_psT], writes=[b_qr8])

    def emit_invrs_bcast():
        p.op(V, lambda v: v.reciprocal(ec[:, 16:24], rscol[:, 0:8]),
             reads=[b_rsc], writes=[b_ec2])
        for c in range(NC):
            p.op(T, (lambda c=c: lambda t: t.matmul(
                ps_x[0:1, 0:128], ec[:, 16 + c:17 + c], idf[:, :],
                start=True, stop=True))(),
                reads=[b_ec2, b_tabs], writes=[b_psX])
            p.op(SC, lambda s2: s2.copy(row[:, 0:128], ps_x[0:1, 0:128]),
                 reads=[b_psX], writes=[b_row])
            p.op(T, lambda t: t.matmul(ps_x[:, 0:128], onr[:, :],
                                       row[:, 0:128], start=True, stop=True),
                 reads=[b_row, b_tabs], writes=[b_psX])
            p.op(SC, (lambda c=c: lambda s2: s2.copy(
                bc[:, 128 * c:128 * c + 128], ps_x[:, 0:128]))(),
                reads=[b_psX], writes=[b_bc])

    for c in range(NC):
        pending.append((lambda c=c: lambda: emit_rhsA(c))())
    pending.append(lambda: p.op(V, lambda v: v.reciprocal(
        rscol[:, 8:16], rscol[:, 0:8]), reads=[b_rsc], writes=[b_rsc]))
    for i in range(NT):
        pending.append((lambda i=i: lambda: emit_xq_v(i))())
        if i > 0:
            pending.append((lambda i=i: lambda: emit_xq_t(i - 1))())
    pending.append(lambda: emit_xq_t(NT - 1))
    for cf in range(NC):
        pending.append((lambda cf=cf: lambda: emit_rhsB_v(cf))())
        if cf > 0:
            pending.append((lambda cf=cf: lambda: emit_rhsB_t(cf - 1))())
    pending.append(lambda: emit_rhsB_t(NC - 1))

    pp = [(ps_b, b_psB), (ps_x, b_psX)]
    ping = [0]
    def nxt():
        ping[0] ^= 1
        return pp[ping[0]]

    def load_head_weights(h):
        dst = wqk[:, 0:1024].rearrange("p (c f) -> p c f", c=8)
        src = d_wq.rearrange("(c p) f -> p c f", p=128)[:, :, 128 * h:128 * h + 128]
        p.dma(G, dst, src, writes=[b_wqk])
        dst = wqk[:, 1024:2048].rearrange("p (c f) -> p c f", c=8)
        src = d_wk.rearrange("(c p) f -> p c f", p=128)[:, :, 128 * h:128 * h + 128]
        p.dma(G, dst, src, writes=[b_wqk])
        dst = wvg[:, 0:2048].rearrange("p (c f) -> p c f", c=8)
        src = d_wv.rearrange("(c p) f -> p c f", p=128)[:, :, 256 * h:256 * h + 256]
        p.dma(G, dst, src, writes=[b_wvg])
        dst = wvg[:, 2048:4096].rearrange("p (c f) -> p c f", c=8)
        src = d_wg.rearrange("(c p) f -> p c f", p=128)[:, :, 256 * h:256 * h + 256]
        p.dma(G, dst, src, writes=[b_wvg])
        p.dma(G, ab[:, 0:S], d_aba[h], writes=[b_ab])
        p.dma(G, ab[:, S:2 * S], d_abb[h], writes=[b_ab])

    load_head_weights(0)
    tp_pending = []
    epi_queue = []

    if stop_stage == "p0":
        return _finish()

    for h in range(H):
        _spg[0] = h >= 2
        for fn in epi_queue:
            fn()
        del epi_queue[:]
        for i2 in range(0, NT, 2):
            ps, bps = nxt()
            def em_vmm(i2=i2, ps=ps):
                def f(t):
                    ins = None
                    for k in range(2):
                        i = i2 + k
                        for c in range(NC):
                            ins = t.matmul(
                                ps[:, 256 * k:256 * k + 256],
                                xn[:, S * c + 128 * i:S * c + 128 * i + 128],
                                wvg[:, 256 * c:256 * c + 256],
                                start=(c == 0), stop=(c == NC - 1))
                    return ins
                return f
            p.op(T, em_vmm(), reads=[b_xng[i2 // 4], b_wvg], writes=[bps])
            p.op(SC, (lambda i2=i2, ps=ps: lambda s: s.copy(
                vr.rearrange("p (s t) -> p s t", s=16)[:, i2:i2 + 2, 0:256],
                ps[:, 0:512].rearrange("p (s t) -> p s t", s=2)))(),
                reads=[bps], writes=[b_vr])
            if i2 == 2 and tp_pending:
                for fn in tp_pending:
                    fn()
                del tp_pending[:]
            if i2 % 4 == 2:
                sprinkle(1)
        for i2 in range(0, NT, 2):
            ps, bps = nxt()
            def em_gmm(i2=i2, ps=ps):
                def f(t):
                    ins = None
                    for k in range(2):
                        i = i2 + k
                        for c in range(NC):
                            ins = t.matmul(
                                ps[:, 256 * k:256 * k + 256],
                                xn[:, S * c + 128 * i:S * c + 128 * i + 128],
                                wvg[:, 2048 + 256 * c:2048 + 256 * c + 256],
                                start=(c == 0), stop=(c == NC - 1))
                    return ins
                return f
            p.op(T, em_gmm(), reads=[b_xng[i2 // 4], b_wvg], writes=[bps])
            p.op(SC, (lambda i2=i2, ps=ps: lambda s: s.activation(
                g1[:, 256 * i2:256 * i2 + 512], ps[:, 0:512], AF.Silu))(),
                reads=[bps], writes=[b_g1])

        for qk in range(2):
            wofs = 1024 * qk
            dstd = qd if qk == 0 else kd
            bdd4 = b_qd4 if qk == 0 else b_kd4
            abo = 0 if qk == 0 else S
            for tg in range(4):
                ps, bps = nxt()
                sslot = tg % 2
                qso = 1024 * sslot
                roto = 1024 * sslot + 512
                bq, br = b_qsb[sslot], b_rot[sslot]
                def em_qmm(tg=tg, ps=ps, wofs=wofs):
                    def f(t):
                        ins = None
                        for c in range(NC):
                            ins = t.matmul(
                                ps[:, 0:512],
                                wqk[:, wofs + 128 * c:wofs + 128 * c + 128],
                                xn[:, S * c + 512 * tg:S * c + 512 * tg + 512],
                                start=(c == 0), stop=(c == NC - 1))
                        return ins
                    return f
                p.op(T, em_qmm(), reads=[b_xng[tg], b_wqk], writes=[bps])
                p.op(SC, (lambda qso=qso, ps=ps: lambda s: s.copy(
                    qsb[:, qso:qso + 512], ps[:, 0:512]))(),
                    reads=[bps], writes=[bq])
                p.op(T, (lambda qso=qso: lambda t: t.matmul(
                    ps_x[:, 0:512], ptb[:, :], qsb[:, qso:qso + 512],
                    start=True, stop=True))(),
                    reads=[bq, b_tabs], writes=[b_psX])
                to = 512 * tg
                rts = rtt[:, 512 * sslot:512 * sslot + 512]
                brt = b_rtt[sslot]
                p.op(V, (lambda qso=qso, to=to, rts=rts: lambda v: v.tensor_mul(
                    rts, qsb[:, qso:qso + 512], cs[:, to:to + 512]))(),
                    reads=[bq, b_cs], writes=[brt])
                p.op(V, (lambda to=to, qso=qso: lambda v: v.tensor_mul(
                    qsb[:, qso:qso + 512], ps_x[:, 0:512],
                    cs[:, S + to:S + to + 512]))(),
                    reads=[b_psX, b_cs], writes=[bq])
                p.op(V, (lambda rts=rts, qso=qso: lambda v: v.tensor_add(
                    rts, rts, qsb[:, qso:qso + 512]))(),
                    reads=[bq], writes=[brt])
                p.op(V, (lambda dstd=dstd, to=to, abo=abo, rts=rts: lambda v:
                         v.tensor_mul(
                    dstd[:, to:to + 512], rts, ab[:, abo + to:abo + to + 512]))(),
                    reads=[brt, b_ab], writes=[bdd4[tg]])
            sprinkle(1)

        for b8 in range(2):
            for k8 in range(8):
                i = 8 * b8 + k8
                p.op(T, (lambda i=i, k8=k8: lambda t: t.transpose(
                    ps_t[:, 128 * k8:128 * k8 + 128],
                    kd[:, 128 * i:128 * i + 128],
                    idb[:, :]))(),
                    reads=[b_kd4[i // 4], b_tabs], writes=[b_psT])
            p.op(SC, (lambda b8=b8: lambda s: s.copy(
                ktm[:, 1024 * b8:1024 * b8 + 1024],
                ps_t[:, 0:1024]))(),
                reads=[b_psT], writes=[b_ktm])

        if h + 1 < H:
            load_head_weights(h + 1)

        zdst = z1 if h < 4 else z2
        for gI in range(4):
            qbase = 512 * gI
            if gI > 0:
                for j in range(4 * (gI - 1), 4 * gI):
                    p.op(T, (lambda j=j: lambda t: t.matmul(
                        ps_a[:, 0:257],
                        ktm[:, 128 * j:128 * j + 128],
                        vr[:, VRS * j:VRS * j + 257],
                        start=(j == 0), stop=(j % 4 == 3),
                        skip_group_check=True))(),
                        reads=[b_ktm, b_vr], writes=[b_psA])
                ssl = (gI - 1) % 2
                st_ap = qsb[:, 512 * ssl:512 * ssl + 257]
                bst = b_qsb[0] if ssl == 0 else b_rot[0]
                p.op(SC, (lambda st_ap=st_ap: lambda s: s.copy(
                    st_ap, ps_a[:, 0:257]))(),
                    reads=[b_psA], writes=[bst])
                def mk_outmm(qbase=qbase, st_ap=st_ap, bst=bst):
                    def go():
                        for ii in range(4):
                            p.op(T, (lambda ii=ii: lambda t: t.matmul(
                                ps_g[:, 512 * ii:512 * ii + 257],
                                qd[:, qbase + 128 * ii:qbase + 128 * ii + 128],
                                st_ap, start=True, stop=False))(),
                                reads=[bst, b_qd4[(qbase // 512)]],
                                writes=[b_psG])
                    return go
                outmm_pending = mk_outmm()
            else:
                outmm_pending = None
            prev_mm2 = None
            for rJ in range(4):
                Jt = 4 * gI + rJ
                co = 128 * rJ
                qoff = qbase + co
                Nq = 512 - co
                ps, bps = nxt()
                p.op(T, (lambda Jt=Jt, qoff=qoff, Nq=Nq, ps=ps: lambda t:
                         t.matmul(ps[:, 0:Nq],
                                  kd[:, 128 * Jt:128 * Jt + 128],
                                  qd[:, qoff:qoff + Nq],
                                  start=True, stop=True))(),
                    reads=[b_kd4[Jt // 4], b_qd4[qoff // 512]],
                    writes=[bps])
                if rJ == 0 and outmm_pending is not None:
                    outmm_pending()
                    outmm_pending = None
                sso = 512 * (rJ % 2)
                bss = b_scsb[rJ % 2]
                p.op(V, (lambda sso=sso, co=co, ps=ps: lambda v:
                         v.tensor_mul(scsb[:, sso + co:sso + co + 128],
                                      ps[:, 0:128], trib[:, :]))(),
                    reads=[bps, b_tabs], writes=[bss])
                if Nq > 128:
                    p.op(SC, (lambda sso=sso, co=co, Nq=Nq, ps=ps:
                              lambda s: s.copy(
                        scsb[:, sso + co + 128:sso + 512],
                        ps[:, 128:Nq]))(),
                        reads=[bps], writes=[bss])
                if prev_mm2 is not None:
                    prev_mm2()
                for _ in range(2):
                    if epi_queue:
                        epi_queue.pop(0)()
                def mk_mm2(sso=sso, bss=bss, Jt=Jt, rJ=rJ, gI=gI):
                    def go():
                        for ii in range(rJ, 4):
                            It = 4 * gI + ii
                            p.op(T, (lambda sso=sso, ii=ii, Jt=Jt, It=It,
                                     gI=gI, rJ=rJ: lambda t: t.matmul(
                                ps_g[:, 512 * ii:512 * ii + 257],
                                scsb[:, sso + 128 * ii:sso + 128 * ii + 128],
                                vr[:, VRS * Jt:VRS * Jt + 257],
                                start=(gI == 0 and rJ == 0),
                                stop=(Jt == It)))(),
                                reads=[bss, b_vr], writes=[b_psG])
                    return go
                prev_mm2 = mk_mm2()
            prev_mm2()
            for fn in tp_pending:
                fn()
            del tp_pending[:]

            gp = gI % 2
            if gp == 0:
                epb, epw = ep, [b_ep]
                ecb, ecw = ec[:, 0:16], [b_ec]
                zsb, zsw = zst, [b_zst]
            else:
                epb, epw = rtt, [b_rtt[0], b_rtt[1]]
                ecb, ecw = ec[:, 16:32], [b_ec2]
                zsb, zsw = qsb[:, 1024:2048], [b_qsb[1], b_rot[1]]
            psg3 = ps_g.rearrange("p (i s) -> p i s", i=4)
            epg3 = epb.rearrange("p (i s) -> p i s", i=4)
            p.op(SC, (lambda epg3=epg3: lambda s: s.copy(
                epg3, psg3[:, :, 0:256]))(),
                reads=[b_psG], writes=epw)
            p.op(SC, (lambda ecb=ecb: lambda s: s.copy(
                ecb[:, 0:4], psg3[:, :, 256:257]))(),
                reads=[b_psG], writes=ecw)

            def mk_epi(epb=epb, epw=epw, ecb=ecb, ecw=ecw, zsb=zsb, zsw=zsw,
                       gI=gI, h=h, zdst=zdst):
                pieces = []
                pieces.append(lambda: (
                    p.op(V, lambda v: v.scalar_tensor_tensor(
                        ecb[:, 0:4], ecb[:, 0:4], -1.0, ecb[:, 0:4],
                        ALU.mult, ALU.max), reads=[], writes=ecw),
                    p.op(V, lambda v: v.tensor_scalar(
                        ecb[:, 0:4], ecb[:, 0:4], 1.0, None, ALU.max),
                        reads=[], writes=ecw),
                    p.op(V, lambda v: v.reciprocal(ecb[:, 4:8], ecb[:, 0:4]),
                         reads=ecw, writes=ecw)))
                pieces.append(lambda: (
                    p.op(V, lambda v: v.tensor_mul(zsb[:, :], epb[:, :],
                                                   epb[:, :]),
                         reads=epw, writes=zsw),
                    p.op(V, lambda v: v.tensor_reduce(
                        ecb[:, 8:12],
                        zsb[:, :].rearrange("p (i s) -> p i s", i=4),
                        axis=AX.X, op=ALU.add),
                        reads=zsw, writes=ecw)))
                pieces.append(lambda: (
                    p.op(V, lambda v: v.tensor_mul(ecb[:, 12:16], ecb[:, 4:8],
                                                   ecb[:, 4:8]),
                         reads=ecw, writes=ecw),
                    p.op(V, lambda v: v.tensor_mul(ecb[:, 12:16],
                                                   ecb[:, 12:16],
                                                   ecb[:, 8:12]),
                         reads=[], writes=ecw),
                    p.op(V, lambda v: v.tensor_scalar(
                        ecb[:, 12:16], ecb[:, 12:16], 1.0 / 256.0, 1e-5,
                        ALU.mult, ALU.add), reads=[], writes=ecw),
                    p.op(V, lambda v: v.reciprocal(ecb[:, 12:16],
                                                   ecb[:, 12:16]),
                         reads=[], writes=ecw),
                    p.op(SC, lambda s: s.activation(ecb[:, 12:16],
                                                    ecb[:, 12:16], AF.Sqrt),
                         reads=ecw, writes=ecw),
                    p.op(V, lambda v: v.tensor_mul(ecb[:, 12:16],
                                                   ecb[:, 12:16],
                                                   ecb[:, 4:8]),
                         reads=[], writes=ecw)))
                for ii in range(4):
                    It = 4 * gI + ii
                    def zpiece(ii=ii, It=It):
                        p.op(V, (lambda: lambda v: v.scalar_tensor_tensor(
                            zsb[:, 256 * ii:256 * ii + 256],
                            epb[:, 256 * ii:256 * ii + 256],
                            ecb[:, 12 + ii:13 + ii],
                            g1[:, 256 * It:256 * It + 256],
                            ALU.mult, ALU.mult))(),
                            reads=epw + ecw + [b_g1], writes=zsw)
                        for hf in range(2):
                            tslot = 2 * ii + hf
                            def mk_tp(hf=hf, tslot=tslot, ii=ii):
                                def go():
                                    p.op(T, (lambda: lambda t: t.transpose(
                                        ps_t[:, 128 * tslot:128 * tslot + 128],
                                        zsb[:, 256 * ii + 128 * hf:
                                             256 * ii + 128 * hf + 128],
                                        idb[:, :]))(),
                                        reads=zsw + [b_tabs], writes=[b_psT])
                                return go
                            tp_pending.append(mk_tp())
                        if ii == 3:
                            def mk_ev():
                                c0 = (2 * h) % 8
                                dst = zdst.rearrange(
                                    "p (c b t) -> p c b t", c=8, b=16)[
                                    :, c0:c0 + 2, 4 * gI:4 * gI + 4, :]
                                src = ps_t[:, 0:1024].rearrange(
                                    "p (b c t) -> p c b t", b=4, c=2)
                                def go():
                                    p.op(SC, (lambda: lambda s: s.copy(
                                        dst, src))(),
                                        reads=[b_psT],
                                        writes=[b_z[2 * h], b_z[2 * h + 1]])
                                return go
                            tp_pending.append(mk_ev())
                    pieces.append(zpiece)
                return pieces
            epi_queue.extend(mk_epi())
            sprinkle(1)
        if debug and h == 1:
            tap("xn1", xn[:, :], b_xng)
            tap("wqk", wqk[:, :], [b_wqk])
            tap("wvg", wvg[:, :], [b_wvg])
            tap("abt", ab[:, :], [b_ab])
            tap("qd", qd[:, :], b_qd4)
            tap("kd", kd[:, :], b_kd4)
            tap("vr", vr[:, :], [b_vr])
            tap("g1", g1[:, :], [b_g1])
            tap("ep", ep[:, :], [b_ep])
            tap("ec", ec[:, :], [b_ec])
            tap("zst", zst[:, :], [b_zst])
            tap("z1h0", z1[:, :], [b_z[0], b_z[1]])
        if stop_stage == f"h{h}":
            return _finish()

    while pending:
        pending.pop(0)()

    for fn in epi_queue:
        fn()
    del epi_queue[:]
    for fn in tp_pending:
        fn()
    del tp_pending[:]
    tap("qx8", qx8[:, :], [b_qx8])
    tap("qr8", qr8[:, :], [b_qr8])
    tap("z1", z1[:, :], b_z[:8])
    tap("z2", z2[:, :], b_z[8:])

    for c in range(16):
        p.dma(G, wo_b[:, 1024 * c:1024 * c + 1024],
              d_wo[128 * c:128 * c + 128, :],
              writes=(b_xng + [b_wo] if c == 0 else [b_wo]))
    def emit_rs_bcast():
        for c in range(NC):
            p.op(T, (lambda c=c: lambda t: t.matmul(
                ps_x[0:1, 0:128], rscol[:, c:c + 1], idf[:, :],
                start=True, stop=True))(),
                reads=[b_rsc, b_tabs], writes=[b_psX])
            p.op(SC, lambda s2: s2.copy(row[:, 0:128], ps_x[0:1, 0:128]),
                 reads=[b_psX], writes=[b_row])
            p.op(T, lambda t: t.matmul(ps_x[:, 0:128], onr[:, :],
                                       row[:, 0:128], start=True, stop=True),
                 reads=[b_row, b_tabs], writes=[b_psX])
            p.op(SC, (lambda c=c: lambda s2: s2.copy(
                rsb2[:, 128 * c:128 * c + 128], ps_x[:, 0:128]))(),
                reads=[b_psX], writes=[b_bc])
    emit_rs_bcast()
    for c in range(NC):
        p.op(V, (lambda c=c: lambda v: v.tensor_scalar(
            qrb[:, 1024 * c:1024 * c + 1024],
            qr8[:, 1024 * c:1024 * c + 1024], -128.0, None, ALU.add))(),
            reads=[b_qr8],
            writes=([b_vr, b_g1, b_qrb] if c == 0 else [b_qrb]))

    b_fin = [Buf("fin0"), Buf("fin1")]
    b_fot = [Buf("fot0"), Buf("fot1")]
    for i in range(NT):
        sl3 = i % 2
        go = 1024 * sl3
        if sl3 == 0:
            xtile = xrl[:, 0:1024]
            bx = [b_xs[0], b_xs[1]]
        else:
            xtile = ufuk[:, 0:1024]
            bx = [b_uf[0], b_uk[0]]
        p.dma(SY, xtile, d_xtm[128 * i:128 * i + 128, :], writes=bx)
        def em_womm(i=i, go=go):
            def f(t):
                ins = None
                for c in range(16):
                    ztens = z1 if c < 8 else z2
                    zc = (c % 8) * S + 128 * i
                    for hf in range(2):
                        ins = t.matmul(
                            ps_g[:, go + 512 * hf:go + 512 * hf + 512],
                            ztens[:, zc:zc + 128],
                            wo_b[:, 1024 * c + 512 * hf:1024 * c + 512 * hf + 512],
                            start=(c == 0), stop=(c == 15))
                return ins
            return f
        bqx = b_qxb[sl3]
        qxv = qxb[:, 1024 * sl3:1024 * sl3 + 1024].rearrange(
            "p (c f) -> p c f", c=8)
        qx8v = qx8.rearrange("p (c t) -> p c t", c=8)[:, :, 128 * i:128 * i + 128]
        p.op(SC, (lambda qxv=qxv, qx8v=qx8v: lambda s2: s2.activation(
            qxv, qx8v, AF.Identity, bias=cc[:, 10:11]))(),
            reads=[b_qx8, b_cc],
            writes=([b_qsb[0], b_qsb[1], b_rot[0], b_rot[1], bqx]
                    if i < 2 else [bqx]))
        for hf, (psf, bpsf) in enumerate(((ps_a, b_psA), (ps_b, b_psB))):
            def em_fmm(i=i, sl3=sl3, hf=hf, psf=psf):
                def f(t):
                    ins = None
                    for c in range(NC):
                        ins = t.matmul(
                            psf[:, 0:512],
                            qxb[:, 1024 * sl3 + 128 * c:1024 * sl3 + 128 * c + 128],
                            qrb[:, 1024 * c + 512 * hf:1024 * c + 512 * hf + 512],
                            start=(c == 0), stop=(c == NC - 1))
                    return ins
                return f
            p.op(T, em_fmm(), reads=[bqx, b_qrb], writes=[bpsf])
        bg2 = b_g2[sl3]
        for hf, (psf, bpsf) in enumerate(((ps_a, b_psA), (ps_b, b_psB))):
            p.op(V, (lambda hf=hf, psf=psf: lambda v: v.tensor_mul(
                fin[:, 512 * hf:512 * hf + 512], psf[:, 0:512],
                rsb2[:, 512 * hf:512 * hf + 512]))(),
                reads=[bpsf, b_bc],
                writes=([b_ab, b_fin[hf]] if i == 0 else [b_fin[hf]]))
            p.op(SC, (lambda hf=hf, sl3=sl3, i=i: lambda s: s.activation(
                g2t[:, 1024 * sl3 + 512 * hf:1024 * sl3 + 512 * hf + 512],
                fin[:, 512 * hf:512 * hf + 512], AF.Silu,
                scale=st2[:, 32 + i:33 + i]))(),
                reads=[b_fin[hf], b_st2],
                writes=(b_qd4 + b_kd4 + [bg2] if i < 2 else [bg2]))
        p.op(T, em_womm(), reads=[b_wo] + b_z, writes=[b_psG])
        for hf in range(2):
            p.op(V, (lambda hf=hf, go=go, sl3=sl3: lambda v: v.tensor_mul(
                fin[:, 512 * hf:512 * hf + 512],
                ps_g[:, go + 512 * hf:go + 512 * hf + 512],
                g2t[:, 1024 * sl3 + 512 * hf:1024 * sl3 + 512 * hf + 512]))(),
                reads=[b_psG, bg2], writes=[b_fin[hf]])
            p.op(G, (lambda hf=hf, sl3=sl3, xtile=xtile: lambda g: g.tensor_add(
                fot[:, 512 * hf:512 * hf + 512],
                fin[:, 512 * hf:512 * hf + 512],
                xtile[:, 512 * hf:512 * hf + 512]))(),
                reads=[b_fin[hf]] + bx,
                writes=([b_ab, b_fot[hf]] if i == 0 else [b_fot[hf]]))
            p.dma(SY, d_out[128 * i:128 * i + 128, 512 * hf:512 * hf + 512],
                  fot[:, 512 * hf:512 * hf + 512], reads=[b_fot[hf]])

    p.raw(G, lambda g: g.nop(), waits=p.dma_total_events())

    with nc.Block() as block:
        p.build(block)
    return nc


_NC_CACHE = None


def kernel(L0_attn_x, L0_attn_wq, L0_attn_wk, L0_attn_wv, L0_attn_wg,
           L0_attn_wo, L0_ffn_rhs):
    global _NC_CACHE
    tabs = _host_tables()
    if _NC_CACHE is None:
        _NC_CACHE = build_nc()
    nc = _NC_CACHE
    x = np.ascontiguousarray(np.asarray(L0_attn_x, dtype=np.float32))
    wq = np.ascontiguousarray(np.asarray(L0_attn_wq, dtype=np.float32))
    wk = np.ascontiguousarray(np.asarray(L0_attn_wk, dtype=np.float32))
    wv = np.ascontiguousarray(np.asarray(L0_attn_wv, dtype=np.float32))
    wg = np.ascontiguousarray(np.asarray(L0_attn_wg, dtype=np.float32))
    wo = np.ascontiguousarray(np.asarray(L0_attn_wo, dtype=np.float32))
    rhs = np.ascontiguousarray(np.asarray(L0_ffn_rhs, dtype=np.float32))
    rhsT = np.ascontiguousarray(rhs.T)
    in_maps = []
    for c in range(NCORES):
        xc = np.ascontiguousarray(x[c])
        in_maps.append(dict(
            x_tm=xc, x_fm=np.ascontiguousarray(xc.T),
            wq=wq, wk=wk, wv=wv, wg=wg, wo=wo, rhs=rhs, rhsT=rhsT,
            cosfm=tabs["cosfm"], sinfm=tabs["sinfm"],
            aba=tabs["aba"], abb=tabs["abb"], Pt=tabs["Pt"],
            tri=tabs["tri"], ident=tabs["ident"]))
    res = run_bass_kernel_spmd(nc, in_maps, core_ids=list(range(NCORES)))
    return np.stack([res.results[c]["out"] for c in range(NCORES)])
